# version 09
# baseline (speedup 1.0000x reference)
import numpy as np
import concourse.bass as bass
import concourse.mybir as mybir
from concourse.alu_op_type import AluOpType as OP
from concourse.bass_utils import run_bass_kernel_spmd

F32 = mybir.dt.float32
BF16 = mybir.dt.bfloat16
AF = mybir.ActivationFunctionType
AX = mybir.AxisListType

D = 1024
H = 1536
G = 3072
NC_D = D // 128
NC_H = H // 128
NC_G = G // 128
E = 8
F = 1024
NT = 2048
ST = 512
NST = 8
STAGE = "full"

ENGS = ("sync", "tensor", "vector", "scalar", "gpsimd")


class Sched:

    def __init__(self):
        self.q = {e: [] for e in ENGS}
        self.n = {}
        self.prod = {}

    def emit(self, eng, fn, waits=(), inc=None, amt=1, tag=""):
        w = []
        for s, c in waits:
            if c is None or c <= 0:
                continue
            have = self.n.get(s, 0)
            assert c <= have, f"wait {s}>={c} but only {have} emitted (eng={eng}, tag={tag})"
            if s in ("t", "v", "a") and self.prod.get(s, set()) == {eng}:
                continue
            w.append((s, c))
        self.q[eng].append((tuple(w), fn, inc, amt, tag))
        if inc is not None:
            self.n[inc] = self.n.get(inc, 0) + amt
            self.prod.setdefault(inc, set()).add(eng)
            return self.n[inc]
        return None

    def replay(self, eng_name, eng, sems):
        self_sem = {"vector": "v", "scalar": "a", "gpsimd": "g"}.get(eng_name)
        own = 0
        last = {}
        for w, fn, inc, amt, tag in self.q[eng_name]:
            if self_sem is not None and own > 0 and inc == self_sem:
                eng.wait_ge(sems[self_sem], own)
            for s, c in w:
                if s == self_sem:
                    continue
                if last.get(s, 0) >= c:
                    continue
                eng.wait_ge(sems[s], c)
                last[s] = c
            inst = fn(eng)
            if inc is not None:
                assert inst is not None, tag
                inst.then_inc(sems[inc], amt)
                if inc == self_sem:
                    own = self.qcount(eng_name, inc, own, amt)

    def qcount(self, eng_name, inc, own, amt):
        return own + amt


def build_nc():
    nc = bass.Bass("TRN2", target_bir_lowering=False)
    dt = nc.dram_tensor

    d_x = dt("xin", [128, NC_D, 2 * NT], F32, kind="ExternalInput").ap()
    d_mask = dt("cmask", [128, 1], F32, kind="ExternalInput").ap()
    d_wi_x = dt("wi_xh", [NC_H, 128, NC_D, 128], F32, kind="ExternalInput").ap()
    d_wi_g = dt("wi_gate", [NC_H, 128, NC_D, 128], F32, kind="ExternalInput").ap()
    d_diag = dt("convdiag", [128, NC_H, 4, 128], F32, kind="ExternalInput").ap()
    d_wg = dt("wgates", [NC_G, 128, NC_H, 128], F32, kind="ExternalInput").ap()
    d_wo2 = dt("wout2", [2 * NC_H, 128, 4, 128], F32, kind="ExternalInput").ap()
    d_fb = dt("fbase", [128, NC_H], F32, kind="ExternalInput").ap()
    d_cb = dt("convb", [128, NC_H], F32, kind="ExternalInput").ap()
    d_bg = dt("bgates", [128, NC_G], F32, kind="ExternalInput").ap()
    d_ones1 = dt("ones1", [1, 128], F32, kind="ExternalInput").ap()
    d_onec = dt("onescol", [128, 1], F32, kind="ExternalInput").ap()
    d_id128 = dt("id128", [128, 128], F32, kind="ExternalInput").ap()
    d_id8 = dt("id8", [8, 8], F32, kind="ExternalInput").ap()
    d_sel8 = dt("sel8", [8, E, 128], F32, kind="ExternalInput").ap()
    d_gw = dt("gatew", [128, NC_D, E], F32, kind="ExternalInput").ap()
    d_gbb = dt("gatebb", [128, E], F32, kind="ExternalInput").ap()
    d_w1g = dt("w1g", [E, E, 128, NC_D, 128], F32, kind="ExternalInput").ap()
    d_w1l = dt("w1l", [E, E, 128, NC_D, 128], F32, kind="ExternalInput").ap()
    d_w3g = dt("w3g", [E, E, 128, NC_D, 128], F32, kind="ExternalInput").ap()
    d_w3l = dt("w3l", [E, E, 128, NC_D, 128], F32, kind="ExternalInput").ap()
    d_w2 = dt("w2", [E, NC_D, 128, E, 128], F32, kind="ExternalInput").ap()
    d_sw1g = dt("sw1g", [E, 128, NC_D, 128], F32, kind="ExternalInput").ap()
    d_sw1l = dt("sw1l", [E, 128, NC_D, 128], F32, kind="ExternalInput").ap()
    d_sw3g = dt("sw3g", [E, 128, NC_D, 128], F32, kind="ExternalInput").ap()
    d_sw3l = dt("sw3l", [E, 128, NC_D, 128], F32, kind="ExternalInput").ap()
    d_sw2 = dt("sw2", [NC_D, 128, E, 128], F32, kind="ExternalInput").ap()
    d_b1g = dt("b1g", [128, E, E], F32, kind="ExternalInput").ap()
    d_b1l = dt("b1l", [128, E, E], F32, kind="ExternalInput").ap()
    d_b3g = dt("b3g", [128, E, E], F32, kind="ExternalInput").ap()
    d_b3l = dt("b3l", [128, E, E], F32, kind="ExternalInput").ap()
    d_b2t = dt("b2t", [E, E, 128], F32, kind="ExternalInput").ap()
    d_sb1g = dt("sb1g", [128, E], F32, kind="ExternalInput").ap()
    d_sb1l = dt("sb1l", [128, E], F32, kind="ExternalInput").ap()
    d_sb3g = dt("sb3g", [128, E], F32, kind="ExternalInput").ap()
    d_sb3l = dt("sb3l", [128, E], F32, kind="ExternalInput").ap()
    d_sb2 = dt("sb2", [128, E], F32, kind="ExternalInput").ap()

    d_out = dt("out", [128, NC_D, NT], F32, kind="ExternalOutput").ap()

    S = Sched()

    from contextlib import ExitStack
    ctx = ExitStack()
    sb = lambda name, shape, dtp: ctx.enter_context(nc.sbuf_tensor(name, shape, dtp))
    ps = lambda name, shape: ctx.enter_context(nc.psum_tensor(name, shape, F32))

    x1 = sb("x1", [128, NC_D, NT], F32)
    c_mask = sb("c_mask", [128, 1], F32)
    c_cvec = sb("c_cvec", [128, NC_H], F32)
    c_cb = sb("c_cb", [128, NC_H], F32)
    c_bg = sb("c_bg", [128, NC_G], F32)
    c_ones1 = sb("c_ones1", [1, 128], F32)
    c_onec = sb("c_onec", [128, 1], BF16)
    c_id128 = sb("c_id128", [128, 128], F32)
    c_id8 = sb("c_id8", [8, 8], F32)
    c_gbb = sb("c_gbb", [128, E], F32)
    c_b1g = sb("c_b1g", [128, E, E], F32)
    c_b1l = sb("c_b1l", [128, E, E], F32)
    c_b3g = sb("c_b3g", [128, E, E], F32)
    c_b3l = sb("c_b3l", [128, E, E], F32)
    c_sb1g = sb("c_sb1g", [128, E], F32)
    c_sb1l = sb("c_sb1l", [128, E], F32)
    c_sb3g = sb("c_sb3g", [128, E], F32)
    c_sb3l = sb("c_sb3l", [128, E], F32)
    c_sb2 = sb("c_sb2", [128, E], F32)
    row_m = sb("row_m", [1, ST], F32)
    row_q = sb("row_q", [1, ST], F32)
    row_r = sb("row_r", [1, ST], F32)
    row_mr = sb("row_mr", [1, ST], F32)

    pb = [ps(f"pb{i}", [128, 512]) for i in range(8)]
    bank_guard = {}

    def t_mm(out_ap, lhsT, rhs, start, stop, waits=(), is_tp=False,
             force_inc=False, tag="mm"):
        if is_tp:
            return S.emit("tensor",
                          lambda t, o=out_ap, l=lhsT, r=rhs: t.transpose(o, l, r),
                          waits=waits, inc="t", tag=tag)
        return S.emit("tensor",
                      lambda t, o=out_ap, l=lhsT, r=rhs, s1=start, s2=stop:
                      t.matmul(o, l, r, start=s1, stop=s2),
                      waits=waits, inc=("t" if (stop or force_inc) else None), tag=tag)

    def bank_wait(b):
        g = bank_guard.get(b)
        if not g:
            return []
        return list(g) if isinstance(g, list) else [g]

    def v_op(fn, waits=(), tag="v"):
        return S.emit("vector", fn, waits=waits, inc="v", tag=tag)

    def a_op(fn, waits=(), tag="a"):
        return S.emit("scalar", fn, waits=waits, inc="a", tag=tag)

    def g_op(fn, waits=(), tag="g"):
        return S.emit("gpsimd", fn, waits=waits, inc="g", tag=tag)

    def g_dma(out_ap, in_ap, waits=(), sem="c", tag="wload"):
        return S.emit("gpsimd",
                      lambda g, o=out_ap, i=in_ap: g.dma_start(out=o, in_=i),
                      waits=waits, inc=sem, amt=16, tag=tag)

    def s_dma_out(out_ap, in_ap, waits=(), tag="store"):
        return S.emit("sync",
                      lambda s, o=out_ap, i=in_ap: s.dma_start(out=o, in_=i),
                      waits=waits, inc="o", amt=16, tag=tag)

    class WStream:

        def __init__(self, slots, tag, sem):
            self.slots = slots
            self.n = len(slots)
            self.tag = tag
            self.sems = tuple(f"{sem}{j}" for j in range(self.n))
            self.nload = 0
            self.ready = {}
            self.release = {}

        def load(self, src_ap):
            i = self.nload
            waits = []
            if i >= self.n:
                assert (i - self.n) in self.release, \
                    f"{self.tag}: load {i} before release of {i - self.n}"
                waits.append(self.release[i - self.n])
            self.ready[i] = g_dma(self.slots[i % self.n][:], src_ap, waits=waits,
                                  sem=self.sems[i % self.n], tag=f"{self.tag}{i}")
            self.nload += 1
            return i

        def slot(self, i):
            return self.slots[i % self.n]

        def wait(self, i):
            return (self.sems[i % self.n], self.ready[i])

        def done(self, i):
            self.release[i] = ("t", S.n["t"])

    with ctx:
        S.n["t"] = 0
        S.prod["t"] = {"tensor"}

        g_dma(c_mask[:], d_mask, tag="c")
        g_dma(c_cvec[:], d_fb, tag="c")
        g_dma(c_cb[:], d_cb, tag="c")
        g_dma(c_bg[:], d_bg, tag="c")
        g_dma(c_ones1[:], d_ones1, tag="c")
        g_dma(c_onec[:], d_onec, tag="c")
        g_dma(c_id128[:], d_id128, tag="c")
        g_dma(c_id8[:], d_id8, tag="c")
        g_dma(c_gbb[:], d_gbb, tag="c")
        g_dma(c_b1g[:], d_b1g, tag="c")
        g_dma(c_b1l[:], d_b1l, tag="c")
        g_dma(c_b3g[:], d_b3g, tag="c")
        g_dma(c_b3l[:], d_b3l, tag="c")
        g_dma(c_sb1g[:], d_sb1g, tag="c")
        g_dma(c_sb1l[:], d_sb1l, tag="c")
        g_dma(c_sb3g[:], d_sb3g, tag="c")
        g_dma(c_sb3l[:], d_sb3l, tag="c")
        g_dma(c_sb2[:], d_sb2, tag="c")

        def lngate(srcs, sqs, ten_after, tag, src_waits=None, sq_waits=None,
                   banks=(4, 5, 6, 7)):
            sb0, sb1, bb0, bb1 = banks
            n = len(srcs)
            for c in range(n):
                w = list(src_waits[c]) if src_waits else ([ten_after] if c == 0 else [])
                t_mm(pb[sb0][0:1, :], c_onec[:], srcs[c],
                     start=(c == 0), stop=(c == n - 1),
                     waits=(bank_wait(sb0) if c == 0 else []) + w,
                     tag=f"{tag}sx")
            for c in range(n):
                w = list(sq_waits[c]) if sq_waits else []
                t_mm(pb[sb1][0:1, :], c_onec[:], sqs[c],
                     start=(c == 0), stop=(c == n - 1),
                     waits=(bank_wait(sb1) if c == 0 else []) + w,
                     tag=f"{tag}sq")
            tcnt = S.n["t"]
            inv = 1.0 / (128 * n)
            v_op(lambda v: v.tensor_scalar(out=row_m[:], in0=pb[sb0][0:1, :],
                                           scalar1=inv, scalar2=None, op0=OP.mult),
                 waits=[("t", tcnt)], tag=f"{tag}m")
            v_op(lambda v: v.tensor_scalar(out=row_q[:], in0=pb[sb1][0:1, :],
                                           scalar1=inv, scalar2=None, op0=OP.mult),
                 tag=f"{tag}q")
            bank_guard[sb0] = ("v", S.n["v"])
            bank_guard[sb1] = ("v", S.n["v"])
            v_op(lambda v: v.tensor_tensor(out=row_mr[:], in0=row_m[:],
                                           in1=row_m[:], op=OP.mult),
                 tag=f"{tag}m2")
            v_op(lambda v: v.scalar_tensor_tensor(out=row_q[:], in0=row_q[:],
                                                  scalar=1e-5, in1=row_mr[:],
                                                  op0=OP.add, op1=OP.subtract),
                 tag=f"{tag}var")
            vcnt = S.n["v"]
            a_op(lambda a: a.activation(out=row_r[:], in_=row_q[:], func=AF.Ln),
                 waits=[("v", vcnt)], tag=f"{tag}ln")
            a_op(lambda a: a.activation(out=row_r[:], in_=row_r[:], func=AF.Exp,
                                        scale=-0.5), tag=f"{tag}ex")
            acnt = S.n["a"]
            v_op(lambda v: v.tensor_tensor(out=row_mr[:], in0=row_m[:],
                                           in1=row_r[:], op=OP.mult),
                 waits=[("a", acnt)], tag=f"{tag}mr")
            vcnt = S.n["v"]
            t_mm(pb[bb0][:, :], c_ones1[:], row_r[:], start=True, stop=True,
                 waits=bank_wait(bb0) + [("v", vcnt)], tag=f"{tag}bcr")
            t_mm(pb[bb1][:, :], c_ones1[:], row_mr[:], start=True, stop=True,
                 waits=bank_wait(bb1), tag=f"{tag}bcmr")
            return S.n["t"]

        with ExitStack() as attn_ctx:
            asb = lambda name, shape, dtp: attn_ctx.enter_context(
                nc.sbuf_tensor(name, shape, dtp))
            c_diag = asb("c_diag", [128, NC_H, 4, 128], BF16)
            xhpre = asb("xhpre", [128, NC_H, 3 + ST], BF16)
            hcol = asb("hcol", [128, NC_H], F32)
            sxbf2 = asb("sxbf2", [128, 2, NC_D, ST], BF16)
            sxnbf = asb("sxnbf", [128, NC_D, ST], BF16)
            ssqbf2 = asb("ssqbf2", [128, 2, NC_D, ST], BF16)
            sgeg = asb("sgeg", [128, NC_H, ST], BF16)
            sxh = asb("sxh", [128, NC_H, ST], BF16)
            ssf = asb("ssf", [128, 2, ST], F32)
            ssi = asb("ssi", [128, 2, ST], F32)
            sa2 = asb("sa2", [128, 2, ST], F32)
            sao = asb("sao", [128, NC_D, ST], F32)
            wi = WStream([asb(f"wi{j}", [128, NC_D, 128], BF16) for j in range(2)], "wi", "wi")
            wg = WStream([asb(f"wg{j}", [128, NC_H, 128], BF16) for j in range(3)], "wg", "wg")
            wo = WStream([asb(f"wo{j}", [128, 4, 128], BF16) for j in range(4)], "wo", "wo")

            cw_cnt = g_dma(c_diag[:], d_diag, tag="diag")
            wc = cw_cnt
            a_op(lambda a: a.activation(out=c_cvec[:], in_=c_cvec[:], func=AF.Exp),
                 waits=[("c", wc)], tag="sp_exp")
            v_op(lambda v: v.tensor_scalar(out=c_cvec[:], in0=c_cvec[:], scalar1=1.0,
                                           scalar2=None, op0=OP.add),
                 waits=[("a", S.n["a"])], tag="sp_add")
            a_op(lambda a: a.activation(out=c_cvec[:], in_=c_cvec[:], func=AF.Ln),
                 waits=[("v", S.n["v"])], tag="sp_ln")
            v_op(lambda v: v.tensor_scalar(out=c_cvec[:], in0=c_cvec[:], scalar1=-8.0,
                                           scalar2=None, op0=OP.mult),
                 waits=[("a", S.n["a"])], tag="cvec")
            v_op(lambda v: v.memset(xhpre[:, :, 0:3], 0.0), tag="haloz")
            v_op(lambda v: v.memset(hcol[:], 0.0), tag="hcolz")

            guard = {}

            import os as _os
            _nst = int(_os.environ.get("KDBG_NST", NST))
            pre = {}
            pre[0] = {"xld": g_dma(sxbf2[:, 0], d_x[:, :, 0:ST], sem="x0", tag="x0"),
                      "xsem": "x0"}
            sq_v0 = []
            for c in range(NC_D):
                v_op(lambda v, c=c: v.tensor_tensor(out=ssqbf2[:, 0, c, :],
                                                    in0=sxbf2[:, 0, c, :],
                                                    in1=sxbf2[:, 0, c, :], op=OP.mult),
                     waits=([("x0", pre[0]["xld"])]) if c == 0 else [],
                     tag="sq0")
                sq_v0.append([("v", S.n["v"])])
            pre[0]["sq_v"] = sq_v0

            for st in range(_nst):
                main = st >= 4
                T0 = st * ST
                par = st % 2
                sxbf = sxbf2[:, par]
                ssqbf = ssqbf2[:, par]

                xld = pre[st]["xld"]
                sq_v = pre[st]["sq_v"]
                xw = [(pre[st]["xsem"], xld), ("c", wc)]
                stats_done = lngate([sxbf[:, c, :] for c in range(NC_D)],
                                    [ssqbf[:, c, :] for c in range(NC_D)],
                                    None, f"ln0_{st}",
                                    src_waits=[xw] + [[]] * (NC_D - 1),
                                    sq_waits=sq_v)
                xnw = [guard["sxnbf"]] if "sxnbf" in guard else []
                xn_v = {}
                for c in range(NC_D):
                    v_op(lambda v, c=c, sxbf=sxbf, ssqbf=ssqbf: v.tensor_tensor(
                            out=ssqbf[:, c, :], in0=sxbf[:, c, :],
                            in1=pb[6][:, :], op=OP.mult),
                         waits=([("t", stats_done)] if c == 0 else []),
                         tag=f"xna{st}")
                    v_op(lambda v, c=c, ssqbf=ssqbf: v.tensor_tensor(
                            out=sxnbf[:, c, :], in0=ssqbf[:, c, :],
                            in1=pb[7][:, :], op=OP.subtract),
                         waits=xnw if c == 0 else [], tag=f"xnb{st}")
                    xn_v[c] = ("v", S.n["v"])
                xn_done = ("v", S.n["v"])
                guard[("ssqbf", par)] = xn_done
                if not main:
                    guard[("sxbf", par)] = xn_done
                bank_guard[6] = xn_done
                bank_guard[7] = xn_done
                if st + 1 < _nst:
                    np_ = (st + 1) % 2
                    lw = [g for g in (guard.get(("sxbf", np_)),) if g]
                    nxld = g_dma(sxbf2[:, np_], d_x[:, :, T0 + ST:T0 + 2 * ST],
                                 waits=lw, sem=f"x{np_}", tag=f"x{st + 1}")
                    sqw = [g for g in (guard.get(("ssqbf", np_)),) if g]
                    nsq = []
                    for c in range(NC_D):
                        v_op(lambda v, c=c, np_=np_: v.tensor_tensor(
                                out=ssqbf2[:, np_, c, :],
                                in0=sxbf2[:, np_, c, :],
                                in1=sxbf2[:, np_, c, :], op=OP.mult),
                             waits=([(f"x{np_}", nxld)] + sqw) if c == 0 else [],
                             tag=f"sq{st + 1}")
                        nsq.append([("v", S.n["v"])])
                    pre[st + 1] = {"xld": nxld, "sq_v": nsq, "xsem": f"x{np_}"}

                if st > 0:
                    rollw = [guard["xhpre_roll"]] if "xhpre_roll" in guard else []
                    v_op(lambda v: v.tensor_copy(out=xhpre[:, :, 0:3],
                                                 in_=xhpre[:, :, ST:ST + 3]),
                         waits=rollw, tag=f"roll{st}")
                roll_done = ("v", S.n["v"])

                halves = [(d_wi_x, "xh")] + ([(d_wi_g, "gate")] if main else [])
                for src, kind in halves:
                    for m in range(NC_H):
                        li = wi.load(src[m])
                        b = m % 2
                        for k in range(NC_D):
                            t_mm(pb[b][:, :], wi.slot(li)[:, k, :], sxnbf[:, k, :],
                                 start=(k == 0), stop=(k == NC_D - 1),
                                 waits=((bank_wait(b) + [wi.wait(li)]) if k == 0 else [])
                                       + ([xn_v[k]] if m == 0 and kind == halves[0][1] else []),
                                 tag=f"wi{kind}{st}")
                        wi.done(li)
                        tcnt = S.n["t"]
                        if kind == "xh":
                            v_op(lambda v, m=m, b=b: v.tensor_copy(
                                    out=xhpre[:, m, 3:3 + ST], in_=pb[b][:, :]),
                                 waits=[("t", tcnt), roll_done], tag=f"evxh{st}")
                            bank_guard[b] = ("v", S.n["v"])
                        else:
                            gw_ = [guard["sgeg"]] if (m == 0 and "sgeg" in guard) else []
                            import os as _os3
                            _gf = AF.Sigmoid if _os3.environ.get("KDBG_NOGELU") else AF.Gelu
                            a_op(lambda a, m=m, b=b, _gf=_gf: a.activation(
                                    out=sgeg[:, m, :], in_=pb[b][:, :], func=_gf),
                                 waits=[("t", tcnt)] + gw_, tag=f"evgeg{st}")
                            bank_guard[b] = ("a", S.n["a"])
                dw = [("c", cw_cnt)]
                xh_a = {}
                xhw = [g for k in ("sxh_t", "sxh_v") if (g := guard.get(k))]
                for c in range(NC_H):
                    b = c % 2
                    for dtap in range(4):
                        t_mm(pb[b][:, :], c_diag[:, c, dtap, :],
                             xhpre[:, c, 3 - dtap:3 - dtap + ST],
                             start=(dtap == 0), stop=(dtap == 3),
                             waits=(bank_wait(b) + (dw if c == 0 else [])),
                             tag=f"conv{st}")
                    tcnt = S.n["t"]
                    a_op(lambda a, c=c, b=b: a.activation(
                            out=sxh[:, c, :], in_=pb[b][:, :], func=AF.Identity,
                            bias=c_cb[:, c:c + 1], scale=1.0),
                         waits=[("t", tcnt)] + (xhw if c == 0 else []),
                         tag=f"evconv{st}")
                    bank_guard[b] = ("a", S.n["a"])
                    xh_a[c] = ("a", S.n["a"])
                guard["xhpre_roll"] = ("t", S.n["t"])
                xh_ready = ("a", S.n["a"])

                geg_a = S.n["a"]
                if st == 4:
                    v_op(lambda v: v.tensor_scalar(out=hcol[:], in0=hcol[:],
                                                   scalar1=c_mask[:], scalar2=None,
                                                   op0=OP.mult),
                         waits=[("c", wc)], tag="maskcarry")
                def midchain(mi, main):
                    pp = mi % 2
                    sig_done = ("a", sig_cnt[mi])
                    v_op(lambda v, pp=pp, mi=mi: v.tensor_scalar(
                            out=ssf[:, pp, :], in0=ssf[:, pp, :],
                            scalar1=c_cvec[:, mi:mi + 1], scalar2=None, op0=OP.mult),
                         waits=[sig_done], tag=f"la{st}")
                    vc = S.n["v"]
                    a_op(lambda a, pp=pp: a.activation(
                            out=ssf[:, pp, :], in_=ssf[:, pp, :], func=AF.Exp),
                         waits=[("v", vc)], tag=f"alpha{st}")
                    al_done = ("a", S.n["a"])
                    v_op(lambda v, pp=pp: v.tensor_tensor(
                            out=sa2[:, pp, :], in0=ssf[:, pp, :], in1=ssf[:, pp, :],
                            op=OP.mult), waits=[al_done], tag=f"a2{st}")
                    v_op(lambda v, pp=pp: v.tensor_scalar(
                            out=sa2[:, pp, :], in0=sa2[:, pp, :], scalar1=-1.0,
                            scalar2=1.0 + 1e-6, op0=OP.mult, op1=OP.add),
                         tag=f"y{st}")
                    return S.n["v"]

                def midtail(mi, main):
                    pp = mi % 2
                    beta_done = ("a", beta_cnt[mi])
                    v_op(lambda v, pp=pp: v.tensor_tensor(
                            out=ssi[:, pp, :], in0=sa2[:, pp, :], in1=ssi[:, pp, :],
                            op=OP.mult), waits=[beta_done], tag=f"bsi{st}")
                    v_op(lambda v, pp=pp, mi=mi: v.tensor_tensor(
                            out=ssi[:, pp, :], in0=ssi[:, pp, :], in1=sxh[:, mi, :],
                            op=OP.mult), tag=f"xs{st}")
                    v_op(lambda v, pp=pp, mi=mi: v.tensor_tensor_scan(
                            out=sa2[:, pp, :], data0=ssf[:, pp, :], data1=ssi[:, pp, :],
                            initial=hcol[:, mi:mi + 1], op0=OP.mult, op1=OP.add),
                         tag=f"scan{st}")
                    v_op(lambda v, pp=pp, mi=mi: v.tensor_copy(
                            out=hcol[:, mi:mi + 1], in_=sa2[:, pp, ST - 1:ST]),
                         tag=f"hc{st}")
                    if main:
                        v_op(lambda v, pp=pp, mi=mi: v.tensor_tensor(
                                out=sgeg[:, mi, :], in0=sgeg[:, mi, :],
                                in1=sa2[:, pp, :], op=OP.mult),
                             waits=[("a", geg_a)], tag=f"gh{st}")
                        gh_v[mi] = ("v", S.n["v"])
                    guard[("mid", pp)] = ("v", S.n["v"])

                sig_cnt = {}
                beta_cnt = {}
                gh_v = {}
                for mi in range(NC_H):
                    pp = mi % 2
                    for sub in range(2):
                        morig = mi if sub == 0 else NC_H + mi
                        li = wg.load(d_wg[morig])
                        b = sub
                        for k in range(NC_H):
                            t_mm(pb[b][:, :], wg.slot(li)[:, k, :], sxh[:, k, :],
                                 start=(k == 0), stop=(k == NC_H - 1),
                                 waits=((bank_wait(b) + [wg.wait(li)]) if k == 0 else [])
                                       + ([xh_a[k]] if mi == 0 and sub == 0 else []),
                                 tag=f"wg{st}")
                        wg.done(li)
                        tcnt = S.n["t"]
                        dst = ssf if sub == 0 else ssi
                        dguard = guard.get(("mid", pp))
                        a_op(lambda a, dst=dst, pp=pp, b=b, morig=morig: a.activation(
                                out=dst[:, pp, :], in_=pb[b][:, :], func=AF.Sigmoid,
                                bias=c_bg[:, morig:morig + 1], scale=1.0),
                             waits=[("t", tcnt)] + ([dguard] if dguard else []),
                             tag=f"evg{st}")
                        bank_guard[b] = ("a", S.n["a"])
                    sig_cnt[mi] = S.n["a"]
                    if pp == 1:
                        yv = {}
                        for mj in (mi - 1, mi):
                            yv[mj] = midchain(mj, main)
                        for mj in (mi - 1, mi):
                            a_op(lambda a, ppj=mj % 2: a.activation(
                                    out=sa2[:, ppj, :], in_=sa2[:, ppj, :], func=AF.Ln),
                                 waits=[("v", yv[mj])], tag=f"lny{st}")
                        for mj in (mi - 1, mi):
                            a_op(lambda a, ppj=mj % 2: a.activation(
                                    out=sa2[:, ppj, :], in_=sa2[:, ppj, :], func=AF.Exp,
                                    scale=0.5), tag=f"beta{st}")
                            beta_cnt[mj] = S.n["a"]
                        for mj in (mi - 1, mi):
                            midtail(mj, main)
                guard["sxh_t"] = ("t", S.n["t"])
                guard["sxh_v"] = ("v", S.n["v"])

                if not main:
                    continue
                import os as _os2
                if _os2.environ.get("KDBG_NOWOUT"):
                    continue

                for pss in range(2):
                    lis = {kk: wo.load(d_wo2[pss * NC_H + kk]) for kk in range(4)}
                    for k in range(NC_H):
                        for mm4 in range(4):
                            t_mm(pb[mm4][:, :], wo.slot(lis[k])[:, mm4, :],
                                 sgeg[:, k, :],
                                 start=(k == 0), stop=(k == NC_H - 1),
                                 force_inc=(mm4 == 3),
                                 waits=(bank_wait(mm4) if k == 0 else [])
                                       + ([wo.wait(lis[k]), gh_v[k]] if mm4 == 0 else []),
                                 tag=f"wo{st}")
                        wo.done(lis[k])
                        if k + 4 < NC_H:
                            lis[k + 4] = wo.load(d_wo2[pss * NC_H + k + 4])
                    for mm4 in range(4):
                        m = pss * 4 + mm4
                        b = mm4
                        tcnt = S.n["t"]
                        guard["sgeg"] = ("t", tcnt)
                        if _os2.environ.get("KDBG_NOEVAO"):
                            bank_guard[b] = ("t", tcnt)
                            continue
                        aow = [guard["sao"]] if (m == 0 and "sao" in guard) else []
                        a_op(lambda a, m=m, b=b: a.copy(out=sao[:, m, :], in_=pb[b][:, :]),
                             waits=[("t", tcnt)] + aow, tag=f"evao{st}")
                        a_op(lambda a, m=m, b=b: a.copy(out=sxnbf[:, m, :],
                                                        in_=pb[b][:, :]),
                             tag=f"aobf{st}")
                        ao_a = ("a", S.n["a"])
                        v_op(lambda v, m=m, b=b, ssqbf=ssqbf: v.tensor_tensor(
                                out=ssqbf[:, m, :], in0=sxnbf[:, m, :],
                                in1=sxnbf[:, m, :], op=OP.mult),
                             waits=[ao_a], tag=f"aosq{st}")
                        bank_guard[b] = [ao_a]
                stats_done = lngate([sxnbf[:, c, :] for c in range(NC_D)],
                                    [ssqbf[:, c, :] for c in range(NC_D)],
                                    ("v", S.n["v"]), f"ln1_{st}", banks=(2, 3, 2, 3))
                guard["sxnbf"] = ("t", stats_done)
                guard[("ssqbf", par)] = ("t", stats_done)
                Tm = T0 - NT
                for c in range(NC_D):
                    v_op(lambda v, c=c: v.tensor_tensor(
                            out=sao[:, c, :], in0=sao[:, c, :], in1=pb[2][:, :],
                            op=OP.mult),
                         waits=([("t", stats_done)] if c == 0 else []), tag=f"x1a{st}")
                    v_op(lambda v, c=c: v.tensor_tensor(
                            out=sao[:, c, :], in0=sao[:, c, :], in1=pb[3][:, :],
                            op=OP.subtract), tag=f"x1b{st}")
                    v_op(lambda v, c=c, Tm=Tm, sxbf=sxbf: v.tensor_tensor(
                            out=x1[:, c, Tm:Tm + ST], in0=sao[:, c, :],
                            in1=sxbf[:, c, :], op=OP.add), tag=f"x1c{st}")
                guard["sao"] = ("v", S.n["v"])
                guard[("sxbf", par)] = ("v", S.n["v"])
                bank_guard[2] = ("v", S.n["v"])
                bank_guard[3] = ("v", S.n["v"])

            attn_v = S.n["v"]
            attn_a = S.n["a"]
            attn_t = S.n["t"]

        if STAGE == "attn":
            st_cnt = None
            for q in range(4):
                st_cnt = s_dma_out(d_out[:, :, q * ST:(q + 1) * ST],
                                   x1[:, :, q * ST:(q + 1) * ST],
                                   waits=[("v", attn_v)], tag="outx1")
            S.emit("sync", lambda s: None, waits=[("o", st_cnt)], tag="fin")
        else:
            with ExitStack() as ffn_ctx:
                fsb = lambda name, shape, dtp: ffn_ctx.enter_context(
                    nc.sbuf_tensor(name, shape, dtp))
                xn2 = fsb("xn2", [128, NC_D, 2 * ST], BF16)
                racc = fsb("racc", [128, NC_D, 2 * ST], F32)
                a_s = fsb("a_s", [128, E, 2 * ST], BF16)
                xbf2 = lambda c: a_s[:, c, 0:ST]
                sq2 = lambda c: a_s[:, c, ST:2 * ST]
                bufA = fsb("bufA", [128, 2, ST], F32)
                bufB = fsb("bufB", [128, 2, ST], F32)
                bufC = fsb("bufC", [128, 2, ST], F32)
                slgf = fsb("slgf", [8, 2 * ST], F32)
                sltok = fsb("sltok", [128, 8, E], F32)
                scwtok = fsb("scwtok", [128, 8, E], F32)
                scwf = fsb("scwf", [8, 2 * ST], F32)
                scwfbf = fsb("scwfbf", [8, 2 * ST], BF16)
                sebuf = fsb("sebuf", [128, E], F32)
                sscore = fsb("sscore", [128, E], F32)
                sbias = fsb("sbias", [128, E], F32)
                smsk1 = fsb("smsk1", [128, E], F32)
                smsk2 = fsb("smsk2", [128, E], F32)
                r1 = fsb("r1", [128, 1], F32)
                r2 = fsb("r2", [128, 1], F32)
                outtile = fsb("outt", [128, NC_D, ST], F32)
                wq = [WStream([fsb(f"wq{q_}{j}", [128, NC_D, 128], BF16)
                               for j in range(2)], f"wq{q_}", f"q{q_}")
                      for q_ in range(4)]
                w2s = WStream([fsb(f"w2s{j}", [128, E, 128], BF16)
                               for j in range(3)], "w2s", "w2")
                c_sel8 = fsb("c_sel8", [8, E, 128], F32)
                c_gw = fsb("c_gw", [128, NC_D, E], BF16)
                c_b2t = fsb("c_b2t", [8, E, 128], BF16)

                bar = [("t", attn_t), ("v", attn_v), ("a", attn_a)]
                for _e in ("tensor", "vector", "scalar", "gpsimd", "sync"):
                    S.emit(_e, lambda e: e.nop(), waits=bar, tag="ffnbar")
                g_dma(c_sel8[:], d_sel8, waits=bar, sem="c2", tag="selc")
                g_dma(c_gw[:], d_gw, waits=[], sem="c2", tag="gwc")
                wc2 = g_dma(c_b2t[:], d_b2t, waits=[], sem="c2", tag="b2c")

                import os as _os
                guard = {}
                store_cnt = 0
                for p in range(2):
                    Tp = p * 2 * ST
                    xn2_v = {}
                    for t in range(2):
                        tsl = slice(Tp + t * ST, Tp + (t + 1) * ST)
                        lsl = slice((2 * p + t) * 0, ST)
                        xg = [guard["xbf2"]] if "xbf2" in guard else []
                        if "xbf2_rd" in guard:
                            xg = xg + [guard["xbf2_rd"]]
                        if "a_s_w2" in guard:
                            xg = xg + [guard["a_s_w2"]]
                        c_v, q_v = {}, {}
                        for c in range(NC_D):
                            a_op(lambda a, c=c, tsl=tsl: a.copy(
                                    out=xbf2(c), in_=x1[:, c, tsl]),
                                 waits=(xg if c == 0 else []), tag=f"l2c{p}{t}")
                            c_v[c] = [("a", S.n["a"])]
                            a_op(lambda a, c=c, tsl=tsl: a.activation(
                                    out=sq2(c), in_=x1[:, c, tsl], func=AF.Square),
                                 tag=f"l2q{p}{t}")
                            q_v[c] = [("a", S.n["a"])]
                        sd = lngate([xbf2(c) for c in range(NC_D)],
                                    [sq2(c) for c in range(NC_D)],
                                    None, f"ln2_{p}{t}",
                                    src_waits=[c_v[c] for c in range(NC_D)],
                                    sq_waits=[q_v[c] for c in range(NC_D)])
                        guard["xbf2"] = ("t", sd)
                        guard["a_s"] = ("t", sd)
                        for c in range(NC_D):
                            v_op(lambda v, c=c: v.tensor_tensor(
                                    out=sq2(c), in0=xbf2(c),
                                    in1=pb[6][:, :], op=OP.mult),
                                 waits=([("t", sd)] if c == 0 else []) + c_v[c],
                                 tag=f"xn2a{p}{t}")
                            v_op(lambda v, c=c, t=t: v.tensor_tensor(
                                    out=xn2[:, c, t * ST:(t + 1) * ST],
                                    in0=sq2(c),
                                    in1=pb[7][:, :], op=OP.subtract),
                                 tag=f"xn2b{p}{t}")
                            xn2_v[(t, c)] = ("v", S.n["v"])
                        guard["xbf2_rd"] = ("v", S.n["v"])

                        bank_guard[6] = ("v", S.n["v"])
                        bank_guard[7] = ("v", S.n["v"])
                    xn2_done = ("v", S.n["v"])

                    cw_blk_v = {}
                    for t in range(2):
                        tloc = slice(t * ST, (t + 1) * ST)
                        for k in range(NC_D):
                            t_mm(pb[4][0:8, :], c_gw[:, k, :], xn2[:, k, tloc],
                                 start=(k == 0), stop=(k == NC_D - 1),
                                 waits=((bank_wait(4) + [("c2", wc2)]) if k == 0 else [])
                                       + [xn2_v[(t, k)]],
                                 tag=f"lg{p}{t}")
                        tc = S.n["t"]
                        v_op(lambda v, t=t, tloc=tloc: v.tensor_copy(
                                out=slgf[0:8, tloc], in_=pb[4][0:8, :]),
                             waits=[("t", tc)], tag=f"lgev{p}{t}")
                        bank_guard[4] = ("v", S.n["v"])
                        lg_done = ("v", S.n["v"])
                        for blk in range(4):
                            t_mm(pb[5][:, blk * 8:(blk + 1) * 8],
                                 slgf[0:8, t * ST + blk * 128: t * ST + (blk + 1) * 128],
                                 c_id8[:], start=True, stop=True,
                                 waits=(bank_wait(5) + [lg_done]), is_tp=True,
                                 tag=f"lgtp{p}{t}")
                        tc = S.n["t"]
                        v_op(lambda v, t=t: v.tensor_copy(
                                out=sltok[:, 4 * t:4 * t + 4, :],
                                in_=pb[5][:, 0:32]),
                             waits=[("t", tc)], tag=f"ltok{p}{t}")
                        bank_guard[5] = ("v", S.n["v"])
                        for blk in range(4):
                            bi = 4 * t + blk
                            v_op(lambda v, bi=bi: v.tensor_reduce(
                                    out=r1[:], in_=sltok[:, bi, :], axis=AX.X,
                                    op=OP.max), tag=f"rmax{p}{bi}")
                            v_op(lambda v: v.tensor_scalar(
                                    out=r2[:], in0=r1[:], scalar1=-1.0,
                                    scalar2=None, op0=OP.mult), tag=f"rng{p}{bi}")
                            vc = S.n["v"]
                            a_op(lambda a, bi=bi: a.activation(
                                    out=sebuf[:], in_=sltok[:, bi, :], func=AF.Exp,
                                    bias=r2[:], scale=1.0),
                                 waits=[("v", vc)], tag=f"rexp{p}{bi}")
                            ac = S.n["a"]
                            v_op(lambda v: v.tensor_reduce(
                                    out=r1[:], in_=sebuf[:], axis=AX.X, op=OP.add),
                                 waits=[("a", ac)], tag=f"rsum{p}{bi}")
                            v_op(lambda v: v.reciprocal(out=r1[:], in_=r1[:]),
                                 tag=f"rrec{p}{bi}")
                            v_op(lambda v: v.tensor_scalar(
                                    out=sscore[:], in0=sebuf[:], scalar1=r1[:],
                                    scalar2=None, op0=OP.mult), tag=f"rsc{p}{bi}")
                            v_op(lambda v: v.tensor_tensor(
                                    out=sbias[:], in0=sscore[:], in1=c_gbb[:],
                                    op=OP.add), tag=f"rb{p}{bi}")
                            v_op(lambda v: v.tensor_reduce(
                                    out=r1[:], in_=sbias[:], axis=AX.X,
                                    op=OP.max), tag=f"rm1{p}{bi}")
                            v_op(lambda v: v.tensor_scalar(
                                    out=smsk1[:], in0=sbias[:], scalar1=r1[:],
                                    scalar2=None, op0=OP.is_equal),
                                 tag=f"rq1{p}{bi}")
                            v_op(lambda v: v.scalar_tensor_tensor(
                                    out=smsk2[:], in0=smsk1[:], scalar=-1e9,
                                    in1=sbias[:], op0=OP.mult, op1=OP.add),
                                 tag=f"rb2{p}{bi}")
                            v_op(lambda v: v.tensor_reduce(
                                    out=r1[:], in_=smsk2[:], axis=AX.X,
                                    op=OP.max), tag=f"rm2{p}{bi}")
                            v_op(lambda v: v.tensor_scalar(
                                    out=smsk2[:], in0=smsk2[:], scalar1=r1[:],
                                    scalar2=None, op0=OP.is_equal),
                                 tag=f"rq2{p}{bi}")
                            v_op(lambda v: v.tensor_tensor(
                                    out=smsk1[:], in0=smsk1[:], in1=smsk2[:],
                                    op=OP.add), tag=f"rq12{p}{bi}")
                            v_op(lambda v, bi=bi: v.tensor_tensor(
                                    out=scwtok[:, bi, :], in0=smsk1[:],
                                    in1=sscore[:], op=OP.mult), tag=f"rcw{p}{bi}")
                            cw_blk_v[bi] = S.n["v"]
                        for blk in range(4):
                            t_mm(pb[5][0:8, blk * 128:(blk + 1) * 128],
                                 scwtok[:, 4 * t + blk, :], c_id128[:],
                                 start=True, stop=True,
                                 waits=(bank_wait(5) if blk == 0 else [])
                                       + [("v", cw_blk_v[4 * t + blk])],
                                 is_tp=True, tag=f"cwtp{p}{t}")
                        tc = S.n["t"]
                        v_op(lambda v, t=t: v.tensor_copy(
                                out=scwf[0:8, t * ST:(t + 1) * ST],
                                in_=pb[5][0:8, 0:512]),
                             waits=[("t", tc)], tag=f"cwf{p}{t}")
                        v_op(lambda v, t=t: v.tensor_copy(
                                out=scwfbf[0:8, t * ST:(t + 1) * ST],
                                in_=scwf[0:8, t * ST:(t + 1) * ST]),
                             tag=f"cwfb{p}{t}")
                        bank_guard[5] = ("v", S.n["v"])
                    cw_done = ("v", S.n["v"])

                    racc_v = {}
                    for e in range(9):
                        shared = (e == 8)
                        if not shared:
                            for t in range(2):
                                t_mm(pb[6 + t][:, :], c_sel8[0:8, e, :],
                                     scwf[0:8, t * ST:(t + 1) * ST],
                                     start=True, stop=True,
                                     waits=(bank_wait(6 + t)
                                            + ([cw_done, ("c2", wc2)] if t == 0 else [])),
                                     tag=f"cwb{p}{e}")
                        cwb_t = S.n["t"]
                        for m in range(E):
                            ql = []
                            for q_ in range(4):
                                srcs = ([d_w1g, d_w3g, d_w1l, d_w3l][q_][e, m]
                                        if not shared else
                                        [d_sw1g, d_sw3g, d_sw1l, d_sw3l][q_][m])
                                ql.append(wq[q_].load(srcs))
                            for t in range(2):
                                tloc = slice(t * ST, (t + 1) * ST)
                                pp = (m * 2 + t) % 2
                                for q_ in range(4):
                                    for k in range(NC_D):
                                        t_mm(pb[q_][:, :], wq[q_].slot(ql[q_])[:, k, :],
                                             xn2[:, k, tloc],
                                             start=(k == 0), stop=(k == NC_D - 1),
                                             waits=(bank_wait(q_)
                                                    + ([wq[q_].wait(ql[q_])] if k == 0 else [])
                                                    + ([xn2_v[(t, k)]] if (e == 0 and m == 0 and q_ == 0) else [])
                                                    + ([("t", cwb_t)] if (k == 0 and q_ == 0 and t == 0) else [])),
                                             tag=f"q{q_}{p}{e}{m}{t}")
                                tc = S.n["t"]
                                bias = lambda cb, sb2_: (
                                    sb2_[:, m:m + 1] if shared else cb[:, e, m:m + 1])
                                a_op(lambda a, pp=pp, b_=bias(c_b1g, c_sb1g): a.activation(
                                        out=bufA[:, pp, :], in_=pb[0][:, :],
                                        func=AF.Identity, bias=b_, scale=1.0),
                                     waits=[("t", tc)], tag=f"t1{p}{e}{m}{t}")
                                a1 = S.n["a"]
                                v_op(lambda v, pp=pp, b_=bias(c_b3g, c_sb3g): v.scalar_tensor_tensor(
                                        out=bufA[:, pp, :], in0=pb[1][:, :],
                                        scalar=b_, in1=bufA[:, pp, :],
                                        op0=OP.add, op1=OP.mult),
                                     waits=[("t", tc), ("a", a1)], tag=f"gg{p}{e}{m}{t}")
                                a_op(lambda a, pp=pp, b_=bias(c_b1l, c_sb1l): a.activation(
                                        out=bufB[:, pp, :], in_=pb[2][:, :],
                                        func=AF.Identity, bias=b_, scale=1.0),
                                     waits=[("t", tc)], tag=f"t2{p}{e}{m}{t}")
                                a2 = S.n["a"]
                                v_op(lambda v, pp=pp, b_=bias(c_b3l, c_sb3l): v.scalar_tensor_tensor(
                                        out=bufB[:, pp, :], in0=pb[3][:, :],
                                        scalar=b_, in1=bufB[:, pp, :],
                                        op0=OP.add, op1=OP.mult),
                                     waits=[("a", a2)], tag=f"ll{p}{e}{m}{t}")
                                bank_guard[0] = ("a", a1)
                                bank_guard[1] = ("v", S.n["v"] - 1)
                                bank_guard[2] = ("a", a2)
                                bank_guard[3] = ("v", S.n["v"])
                                v_op(lambda v, pp=pp: v.tensor_scalar(
                                        out=bufA[:, pp, :], in0=bufA[:, pp, :],
                                        scalar1=7.0, scalar2=None, op0=OP.min),
                                     tag=f"g7{p}{e}{m}{t}")
                                vc = S.n["v"]
                                a_op(lambda a, pp=pp: a.activation(
                                        out=bufC[:, pp, :], in_=bufA[:, pp, :],
                                        func=AF.Sigmoid, scale=1.702),
                                     waits=[("v", vc)], tag=f"sg{p}{e}{m}{t}")
                                a3 = S.n["a"]
                                v_op(lambda v, pp=pp: v.tensor_tensor(
                                        out=bufA[:, pp, :], in0=bufA[:, pp, :],
                                        in1=bufC[:, pp, :], op=OP.mult),
                                     waits=[("a", a3)], tag=f"p1{p}{e}{m}{t}")
                                v_op(lambda v, pp=pp: v.tensor_scalar(
                                        out=bufB[:, pp, :], in0=bufB[:, pp, :],
                                        scalar1=-7.0, scalar2=7.0,
                                        op0=OP.max, op1=OP.min),
                                     tag=f"lc{p}{e}{m}{t}")
                                if shared:
                                    v_op(lambda v, pp=pp: v.tensor_scalar(
                                            out=bufB[:, pp, :], in0=bufB[:, pp, :],
                                            scalar1=1.0, scalar2=None, op0=OP.add),
                                         tag=f"p2{p}{e}{m}{t}")
                                else:
                                    v_op(lambda v, pp=pp, t=t: v.scalar_tensor_tensor(
                                            out=bufB[:, pp, :], in0=bufB[:, pp, :],
                                            scalar=1.0, in1=pb[6 + t][:, :],
                                            op0=OP.add, op1=OP.mult),
                                         tag=f"p2{p}{e}{m}{t}")
                                asg = [guard["a_s"]] if (e == 0 and m == 0 and t == 0 and "a_s" in guard) else []
                                v_op(lambda v, pp=pp, m=m, tloc=tloc: v.tensor_tensor(
                                        out=a_s[:, m, tloc], in0=bufA[:, pp, :],
                                        in1=bufB[:, pp, :], op=OP.mult),
                                     waits=asg, tag=f"as{p}{e}{m}{t}")
                            for q_ in range(4):
                                wq[q_].done(ql[q_])
                        as_done = ("v", S.n["v"])
                        if not shared:
                            bank_guard[6] = as_done
                            bank_guard[7] = as_done
                        for m2 in range(E):
                            li = w2s.load(d_w2[e, m2] if not shared else d_sw2[m2])
                            for t in range(2):
                                tloc = slice(t * ST, (t + 1) * ST)
                                b = 4 + t
                                for k in range(E):
                                    t_mm(pb[b][:, :], w2s.slot(li)[:, k, :],
                                         a_s[:, k, tloc],
                                         start=(k == 0),
                                         stop=(k == E - 1 and e != 0),
                                         waits=(bank_wait(b)
                                                + ([w2s.wait(li), as_done] if k == 0 else [])),
                                         tag=f"w2{p}{e}{m2}{t}")
                                if e == 0:
                                    t_mm(pb[b][:, :], c_b2t[0:8, m2, :],
                                         scwfbf[0:8, tloc],
                                         start=False, stop=True,
                                         waits=[], tag=f"b2{p}{m2}{t}")
                                tc = S.n["t"]
                                if e == 0:
                                    v_op(lambda v, m2=m2, tloc=tloc, b=b: v.tensor_copy(
                                            out=racc[:, m2, tloc], in_=pb[b][:, :]),
                                         waits=[("t", tc)], tag=f"ra{p}{e}{m2}{t}")
                                elif not shared:
                                    v_op(lambda v, m2=m2, tloc=tloc, b=b: v.tensor_tensor(
                                            out=racc[:, m2, tloc],
                                            in0=racc[:, m2, tloc], in1=pb[b][:, :],
                                            op=OP.add),
                                         waits=[("t", tc)], tag=f"ra{p}{e}{m2}{t}")
                                else:
                                    v_op(lambda v, m2=m2, tloc=tloc, b=b: v.scalar_tensor_tensor(
                                            out=racc[:, m2, tloc], in0=pb[b][:, :],
                                            scalar=c_sb2[:, m2:m2 + 1],
                                            in1=racc[:, m2, tloc],
                                            op0=OP.add, op1=OP.add),
                                         waits=[("t", tc)], tag=f"ra{p}{e}{m2}{t}")
                                bank_guard[b] = ("v", S.n["v"])
                                racc_v[(t, m2)] = S.n["v"]
                            w2s.done(li)
                    guard["a_s_w2"] = ("t", S.n["t"])
                    racc_done = ("v", S.n["v"])

                    if _os.environ.get("KDBG_NOLN3"):
                        if p == 1:
                            store_cnt = s_dma_out(d_out[:, :, 0:ST], racc[:, :, 0:ST],
                                                  waits=[racc_done], tag="dbgout")
                        continue
                    for t in range(2):
                        tsl = slice(Tp + t * ST, Tp + (t + 1) * ST)
                        tloc = slice(t * ST, (t + 1) * ST)
                        l3g = [("t", S.n["t"])]
                        c3_v, q3_v = {}, {}
                        for c in range(NC_D):
                            rw = [("v", racc_v[(t, c)])]
                            a_op(lambda a, c=c, tloc=tloc: a.copy(
                                    out=xbf2(c), in_=racc[:, c, tloc]),
                                 waits=(l3g if c == 0 else []) + rw, tag=f"l3c{p}{t}")
                            c3_v[c] = [("a", S.n["a"])]
                            a_op(lambda a, c=c, tloc=tloc: a.activation(
                                    out=sq2(c), in_=racc[:, c, tloc], func=AF.Square),
                                 tag=f"l3q{p}{t}")
                            q3_v[c] = [("a", S.n["a"])]
                        sd = lngate([xbf2(c) for c in range(NC_D)],
                                    [sq2(c) for c in range(NC_D)],
                                    None, f"ln3_{p}{t}",
                                    src_waits=[c3_v[c] for c in range(NC_D)],
                                    sq_waits=[q3_v[c] for c in range(NC_D)],
                                    banks=(0, 1, 0, 1))
                        og = [guard["outtile"]] if "outtile" in guard else []
                        for c in range(NC_D):
                            v_op(lambda v, c=c, tloc=tloc: v.tensor_tensor(
                                    out=bufA[:, 0, :],
                                    in0=racc[:, c, tloc], in1=pb[0][:, :],
                                    op=OP.mult),
                                 waits=([("t", sd)] if c == 0 else []),
                                 tag=f"o1{p}{t}")
                            v_op(lambda v, c=c: v.tensor_tensor(
                                    out=bufA[:, 0, :], in0=bufA[:, 0, :],
                                    in1=pb[1][:, :], op=OP.subtract),
                                 tag=f"o2{p}{t}")
                            v_op(lambda v, c=c, tsl=tsl: v.tensor_tensor(
                                    out=outtile[:, c, :], in0=bufA[:, 0, :],
                                    in1=x1[:, c, tsl], op=OP.add),
                                 waits=(og if c == 0 else []), tag=f"o3{p}{t}")
                        bank_guard[0] = ("v", S.n["v"])
                        bank_guard[1] = ("v", S.n["v"])
                        ot = S.n["v"]
                        store_cnt = s_dma_out(d_out[:, :, tsl], outtile[:, :, :],
                                              waits=[("v", ot)], tag=f"out{p}{t}")
                        guard["outtile"] = ("o", store_cnt)
                S.emit("sync", lambda s: None, waits=[("o", store_cnt)], tag="fin")

        with ExitStack() as semctx:
            sems = {name: semctx.enter_context(nc.semaphore(f"s_{name}"))
                    for name in S.n}
            block = semctx.enter_context(nc.Block())

            @block.sync
            def _(sync):
                S.replay("sync", sync, sems)

            @block.tensor
            def _(tensor):
                S.replay("tensor", tensor, sems)

            @block.vector
            def _(vector):
                S.replay("vector", vector, sems)

            @block.scalar
            def _(scalar):
                S.replay("scalar", scalar, sems)

            @block.gpsimd
            def _(gpsimd):
                S.replay("gpsimd", gpsimd, sems)

    return nc



def _prep_shared(i):
    W_in = i["L0_attn_W_in"]
    w_in_gate = np.ascontiguousarray(
        W_in[:H].reshape(NC_H, 128, NC_D, 128).transpose(0, 3, 2, 1))
    w_in_xh = np.ascontiguousarray(
        W_in[H:].reshape(NC_H, 128, NC_D, 128).transpose(0, 3, 2, 1))
    cw = i["L0_attn_conv_w"]
    diag = np.zeros((128, NC_H, 4, 128), np.float32)
    kk = np.arange(128)
    for c in range(NC_H):
        for d in range(4):
            diag[kk, c, d, kk] = cw[c * 128:(c + 1) * 128, 3 - d]
    wg = np.ascontiguousarray(
        i["L0_attn_W_gates"].reshape(NC_G, 128, NC_H, 128).transpose(0, 3, 2, 1))
    pair = [v for mi in range(NC_H) for v in (mi, NC_H + mi)]
    wo = np.ascontiguousarray(
        i["L0_attn_W_out"].reshape(2, 4, 128, NC_H, 128)
        .transpose(0, 3, 4, 1, 2).reshape(2 * NC_H, 128, 4, 128))
    fm = lambda v, n: np.ascontiguousarray(v.reshape(n, 128).T)
    quad = lambda w, sl: np.ascontiguousarray(
        w[:, sl, :].reshape(E, E, 128, NC_D, 128).transpose(0, 1, 4, 3, 2))
    squad = lambda w, sl: np.ascontiguousarray(
        w[sl, :].reshape(E, 128, NC_D, 128).transpose(0, 3, 2, 1))
    b_em = lambda b, sl: np.ascontiguousarray(
        b[:, sl].reshape(E, E, 128).transpose(2, 0, 1))
    d = {
        "wi_gate": w_in_gate, "wi_xh": w_in_xh, "convdiag": diag,
        "wgates": wg, "wout2": wo,
        "fbase": fm(i["L0_attn_forget_base"], NC_H),
        "convb": fm(i["L0_attn_conv_b"], NC_H),
        "bgates": fm(i["L0_attn_b_gates"], NC_G),
        "ones1": np.ones((1, 128), np.float32),
        "onescol": np.ones((128, 1), np.float32),
        "id128": np.eye(128, dtype=np.float32),
        "id8": np.eye(8, dtype=np.float32),
        "sel8": np.ascontiguousarray(
            np.eye(8, dtype=np.float32)[:, :, None]
            * np.ones((1, 1, 128), np.float32)),
        "gatew": np.ascontiguousarray(
            i["L0_ffn_gate_w"].reshape(E, NC_D, 128).transpose(2, 1, 0)),
        "gatebb": np.ascontiguousarray(
            np.broadcast_to(i["L0_ffn_gate_b"][None, :], (128, E))),
        "w1g": quad(i["L0_ffn_w1"], slice(0, None, 2)),
        "w1l": quad(i["L0_ffn_w1"], slice(1, None, 2)),
        "w3g": quad(i["L0_ffn_w3"], slice(0, None, 2)),
        "w3l": quad(i["L0_ffn_w3"], slice(1, None, 2)),
        "sw1g": squad(i["L0_ffn_sw1"], slice(0, None, 2)),
        "sw1l": squad(i["L0_ffn_sw1"], slice(1, None, 2)),
        "sw3g": squad(i["L0_ffn_sw3"], slice(0, None, 2)),
        "sw3l": squad(i["L0_ffn_sw3"], slice(1, None, 2)),
        "sw2": np.ascontiguousarray(
            i["L0_ffn_sw2"].reshape(NC_D, 128, E, 128).transpose(0, 3, 2, 1)),
        "b1g": b_em(i["L0_ffn_b1"], slice(0, None, 2)),
        "b1l": b_em(i["L0_ffn_b1"], slice(1, None, 2)),
        "b3g": b_em(i["L0_ffn_b3"], slice(0, None, 2)),
        "b3l": b_em(i["L0_ffn_b3"], slice(1, None, 2)),
        "b2t": np.ascontiguousarray(i["L0_ffn_b2"].reshape(E, NC_D, 128)),
        "sb1g": fm(i["L0_ffn_sb1"][0::2], E), "sb1l": fm(i["L0_ffn_sb1"][1::2], E),
        "sb3g": fm(i["L0_ffn_sb3"][0::2], E), "sb3l": fm(i["L0_ffn_sb3"][1::2], E),
        "sb2": fm(i["L0_ffn_sb2"], NC_D),
    }
    return d


def kernel(**inputs):
    x = np.asarray(inputs["L0_attn_x"])
    shared = _prep_shared(inputs)
    w2 = inputs["L0_ffn_w2"]
    shared["w2"] = np.ascontiguousarray(
        w2.reshape(E, NC_D, 128, E, 128).transpose(0, 1, 4, 3, 2))

    in_maps = []
    for core in range(8):
        seq, half = core // 2, core % 2
        mainx = x[seq, half * NT:(half + 1) * NT]
        prefx = (x[seq, 0:NT] if half == 1
                 else np.zeros((NT, D), np.float32))
        xin = np.concatenate([prefx, mainx], axis=0)
        xin_fm = np.ascontiguousarray(
            xin.T.reshape(NC_D, 128, 2 * NT).transpose(1, 0, 2))
        m = dict(shared)
        m["xin"] = xin_fm
        m["cmask"] = np.full((128, 1), float(half), np.float32)
        in_maps.append(m)

    nc = build_nc()
    res = run_bass_kernel_spmd(nc, in_maps, core_ids=list(range(8)))

    out = np.empty((4, 4096, D), np.float32)
    for core in range(8):
        seq, half = core // 2, core % 2
        o = res.results[core]["out"]
        o2 = np.asarray(o).transpose(1, 0, 2).reshape(D, NT).T
        out[seq, half * NT:(half + 1) * NT] = o2
    return out


# version 10
# speedup vs baseline: 1.1890x; 1.1890x over previous
import numpy as np
import concourse.bass as bass
import concourse.mybir as mybir
from concourse.alu_op_type import AluOpType as OP
from concourse.bass_utils import run_bass_kernel_spmd

F32 = mybir.dt.float32
BF16 = mybir.dt.bfloat16
AF = mybir.ActivationFunctionType
AX = mybir.AxisListType

D = 1024
H = 1536
G = 3072
NC_D = D // 128
NC_H = H // 128
NC_G = G // 128
E = 8
F = 1024
NT = 2048
ST = 512
NST = 8
STAGE = "full"

ENGS = ("sync", "tensor", "vector", "scalar", "gpsimd")


class Sched:

    def __init__(self):
        self.q = {e: [] for e in ENGS}
        self.n = {}
        self.prod = {}

    def emit(self, eng, fn, waits=(), inc=None, amt=1, tag=""):
        w = []
        for s, c in waits:
            if c is None or c <= 0:
                continue
            have = self.n.get(s, 0)
            assert c <= have, f"wait {s}>={c} but only {have} emitted (eng={eng}, tag={tag})"
            if s in ("t", "v", "a") and self.prod.get(s, set()) == {eng}:
                continue
            w.append((s, c))
        self.q[eng].append((tuple(w), fn, inc, amt, tag))
        if inc is not None:
            self.n[inc] = self.n.get(inc, 0) + amt
            self.prod.setdefault(inc, set()).add(eng)
            return self.n[inc]
        return None

    def replay(self, eng_name, eng, sems):
        self_sem = {"vector": "v", "scalar": "a", "gpsimd": "g"}.get(eng_name)
        own = 0
        last = {}
        for w, fn, inc, amt, tag in self.q[eng_name]:
            if self_sem is not None and own > 0 and inc == self_sem:
                eng.wait_ge(sems[self_sem], own)
            for s, c in w:
                if s == self_sem:
                    continue
                if last.get(s, 0) >= c:
                    continue
                eng.wait_ge(sems[s], c)
                last[s] = c
            inst = fn(eng)
            if inc is not None:
                assert inst is not None, tag
                inst.then_inc(sems[inc], amt)
                if inc == self_sem:
                    own = self.qcount(eng_name, inc, own, amt)

    def qcount(self, eng_name, inc, own, amt):
        return own + amt


def build_nc():
    nc = bass.Bass("TRN2", target_bir_lowering=False)
    dt = nc.dram_tensor

    d_x = dt("xin", [128, NC_D, 2 * NT], F32, kind="ExternalInput").ap()
    d_mask = dt("cmask", [128, 1], F32, kind="ExternalInput").ap()
    d_wi_x = dt("wi_xh", [NC_H, 128, NC_D, 128], F32, kind="ExternalInput").ap()
    d_wi_g = dt("wi_gate", [NC_H, 128, NC_D, 128], F32, kind="ExternalInput").ap()
    d_diag = dt("convdiag", [128, NC_H, 4, 128], F32, kind="ExternalInput").ap()
    d_wg = dt("wgates", [NC_G, 128, NC_H, 128], F32, kind="ExternalInput").ap()
    d_wo2 = dt("wout2", [2 * NC_H, 128, 4, 128], F32, kind="ExternalInput").ap()
    d_fb = dt("fbase", [128, NC_H], F32, kind="ExternalInput").ap()
    d_cb = dt("convb", [128, NC_H], F32, kind="ExternalInput").ap()
    d_bg = dt("bgates", [128, NC_G], F32, kind="ExternalInput").ap()
    d_ones1 = dt("ones1", [1, 128], F32, kind="ExternalInput").ap()
    d_onec = dt("onescol", [128, 1], F32, kind="ExternalInput").ap()
    d_onecf = dt("onescolf", [128, 1], F32, kind="ExternalInput").ap()
    d_id128 = dt("id128", [128, 128], F32, kind="ExternalInput").ap()
    d_id8 = dt("id8", [8, 8], F32, kind="ExternalInput").ap()
    d_sel8 = dt("sel8", [8, E, 128], F32, kind="ExternalInput").ap()
    d_gw = dt("gatew", [128, NC_D, E], F32, kind="ExternalInput").ap()
    d_gbb = dt("gatebb", [128, E], F32, kind="ExternalInput").ap()
    d_w1g = dt("w1g", [E, E, 128, NC_D, 128], F32, kind="ExternalInput").ap()
    d_w1l = dt("w1l", [E, E, 128, NC_D, 128], F32, kind="ExternalInput").ap()
    d_w3g = dt("w3g", [E, E, 128, NC_D, 128], F32, kind="ExternalInput").ap()
    d_w3l = dt("w3l", [E, E, 128, NC_D, 128], F32, kind="ExternalInput").ap()
    d_w2 = dt("w2", [E, NC_D, 128, E, 128], F32, kind="ExternalInput").ap()
    d_sw1g = dt("sw1g", [E, 128, NC_D, 128], F32, kind="ExternalInput").ap()
    d_sw1l = dt("sw1l", [E, 128, NC_D, 128], F32, kind="ExternalInput").ap()
    d_sw3g = dt("sw3g", [E, 128, NC_D, 128], F32, kind="ExternalInput").ap()
    d_sw3l = dt("sw3l", [E, 128, NC_D, 128], F32, kind="ExternalInput").ap()
    d_sw2 = dt("sw2", [NC_D, 128, E, 128], F32, kind="ExternalInput").ap()
    d_b1g = dt("b1g", [128, E, E], F32, kind="ExternalInput").ap()
    d_b1l = dt("b1l", [128, E, E], F32, kind="ExternalInput").ap()
    d_b3g = dt("b3g", [128, E, E], F32, kind="ExternalInput").ap()
    d_b3l = dt("b3l", [128, E, E], F32, kind="ExternalInput").ap()
    d_b2t = dt("b2t", [E, E, 128], F32, kind="ExternalInput").ap()
    d_sb1g = dt("sb1g", [128, E], F32, kind="ExternalInput").ap()
    d_sb1l = dt("sb1l", [128, E], F32, kind="ExternalInput").ap()
    d_sb3g = dt("sb3g", [128, E], F32, kind="ExternalInput").ap()
    d_sb3l = dt("sb3l", [128, E], F32, kind="ExternalInput").ap()
    d_sb2 = dt("sb2", [128, E], F32, kind="ExternalInput").ap()

    d_out = dt("out", [128, NC_D, NT], F32, kind="ExternalOutput").ap()

    S = Sched()

    from contextlib import ExitStack
    ctx = ExitStack()
    sb = lambda name, shape, dtp: ctx.enter_context(nc.sbuf_tensor(name, shape, dtp))
    ps = lambda name, shape: ctx.enter_context(nc.psum_tensor(name, shape, F32))

    x1 = sb("x1", [128, NC_D, NT], F32)
    c_mask = sb("c_mask", [128, 1], F32)
    c_cvec = sb("c_cvec", [128, NC_H], F32)
    c_cb = sb("c_cb", [128, NC_H], F32)
    c_bg = sb("c_bg", [128, NC_G], F32)
    c_ones1 = sb("c_ones1", [1, 128], F32)
    c_onec = sb("c_onec", [128, 1], BF16)
    c_onecf = sb("c_onecf", [128, 1], F32)
    c_id128 = sb("c_id128", [128, 128], F32)
    c_id8 = sb("c_id8", [8, 8], F32)
    c_gbb = sb("c_gbb", [128, E], F32)
    c_b1g = sb("c_b1g", [128, E, E], F32)
    c_b1l = sb("c_b1l", [128, E, E], F32)
    c_b3g = sb("c_b3g", [128, E, E], F32)
    c_b3l = sb("c_b3l", [128, E, E], F32)
    c_sb1g = sb("c_sb1g", [128, E], F32)
    c_sb1l = sb("c_sb1l", [128, E], F32)
    c_sb3g = sb("c_sb3g", [128, E], F32)
    c_sb3l = sb("c_sb3l", [128, E], F32)
    c_sb2 = sb("c_sb2", [128, E], F32)
    row_m = sb("row_m", [1, ST], F32)
    row_q = sb("row_q", [1, ST], F32)
    row_r = sb("row_r", [1, ST], F32)
    row_mr = sb("row_mr", [1, ST], F32)

    pb = [ps(f"pb{i}", [128, 512]) for i in range(8)]
    bank_guard = {}

    def t_mm(out_ap, lhsT, rhs, start, stop, waits=(), is_tp=False,
             force_inc=False, tag="mm"):
        if is_tp:
            return S.emit("tensor",
                          lambda t, o=out_ap, l=lhsT, r=rhs: t.transpose(o, l, r),
                          waits=waits, inc="t", tag=tag)
        return S.emit("tensor",
                      lambda t, o=out_ap, l=lhsT, r=rhs, s1=start, s2=stop:
                      t.matmul(o, l, r, start=s1, stop=s2),
                      waits=waits, inc=("t" if (stop or force_inc) else None), tag=tag)

    def bank_wait(b):
        g = bank_guard.get(b)
        if not g:
            return []
        return list(g) if isinstance(g, list) else [g]

    def v_op(fn, waits=(), tag="v"):
        return S.emit("vector", fn, waits=waits, inc="v", tag=tag)

    def a_op(fn, waits=(), tag="a"):
        return S.emit("scalar", fn, waits=waits, inc="a", tag=tag)

    def g_op(fn, waits=(), tag="g"):
        return S.emit("gpsimd", fn, waits=waits, inc="g", tag=tag)

    def g_dma(out_ap, in_ap, waits=(), sem="c", tag="wload"):
        return S.emit("gpsimd",
                      lambda g, o=out_ap, i=in_ap: g.dma_start(out=o, in_=i),
                      waits=waits, inc=sem, amt=16, tag=tag)

    def s_dma_out(out_ap, in_ap, waits=(), tag="store"):
        return S.emit("sync",
                      lambda s, o=out_ap, i=in_ap: s.dma_start(out=o, in_=i),
                      waits=waits, inc="o", amt=16, tag=tag)

    class WStream:

        def __init__(self, slots, tag, sem):
            self.slots = slots
            self.n = len(slots)
            self.tag = tag
            self.sems = tuple(f"{sem}{j}" for j in range(self.n))
            self.nload = 0
            self.ready = {}
            self.release = {}

        def load(self, src_ap):
            i = self.nload
            waits = []
            if i >= self.n:
                assert (i - self.n) in self.release, \
                    f"{self.tag}: load {i} before release of {i - self.n}"
                waits.append(self.release[i - self.n])
            self.ready[i] = g_dma(self.slots[i % self.n][:], src_ap, waits=waits,
                                  sem=self.sems[i % self.n], tag=f"{self.tag}{i}")
            self.nload += 1
            return i

        def slot(self, i):
            return self.slots[i % self.n]

        def wait(self, i):
            return (self.sems[i % self.n], self.ready[i])

        def done(self, i):
            self.release[i] = ("t", S.n["t"])

    with ctx:
        S.n["t"] = 0
        S.prod["t"] = {"tensor"}

        g_dma(c_mask[:], d_mask, tag="c")
        g_dma(c_cvec[:], d_fb, tag="c")
        g_dma(c_cb[:], d_cb, tag="c")
        g_dma(c_bg[:], d_bg, tag="c")
        g_dma(c_ones1[:], d_ones1, tag="c")
        g_dma(c_onec[:], d_onec, tag="c")
        g_dma(c_onecf[:], d_onecf, tag="c")
        g_dma(c_id128[:], d_id128, tag="c")
        g_dma(c_id8[:], d_id8, tag="c")
        g_dma(c_gbb[:], d_gbb, tag="c")
        g_dma(c_b1g[:], d_b1g, tag="c")
        g_dma(c_b1l[:], d_b1l, tag="c")
        g_dma(c_b3g[:], d_b3g, tag="c")
        g_dma(c_b3l[:], d_b3l, tag="c")
        g_dma(c_sb1g[:], d_sb1g, tag="c")
        g_dma(c_sb1l[:], d_sb1l, tag="c")
        g_dma(c_sb3g[:], d_sb3g, tag="c")
        g_dma(c_sb3l[:], d_sb3l, tag="c")
        g_dma(c_sb2[:], d_sb2, tag="c")

        def lngate(srcs, sqs, ten_after, tag, src_waits=None, sq_waits=None,
                   banks=(4, 5, 6, 7), fp32=False, sq_gen=None):
            sb0, sb1, bb0, bb1 = banks
            onesv = c_onecf if fp32 else c_onec
            n = len(srcs) if srcs is not None else len(sqs)
            if srcs is not None:
                for c in range(n):
                    w = (list(src_waits[c]) if src_waits
                         else ([ten_after] if c == 0 else []))
                    t_mm(pb[sb0][0:1, :], onesv[:], srcs[c],
                         start=(c == 0), stop=(c == n - 1),
                         waits=(bank_wait(sb0) if c == 0 else []) + w,
                         tag=f"{tag}sx")
            if sq_gen is not None:
                rel = {}
                for c in range(n):
                    gw = [("t", rel[c - 2])] if (c - 2) in rel else (
                        list(sq_waits[c]) if sq_waits else [])
                    ap = sq_gen(c, gw + (list(sq_waits[c]) if sq_waits and c >= 2 else []))
                    aw = [("a", S.n["a"])]
                    t_mm(pb[sb1][0:1, :], onesv[:], ap,
                         start=(c == 0), stop=(c == n - 1), force_inc=True,
                         waits=(bank_wait(sb1) if c == 0 else []) + aw,
                         tag=f"{tag}sq")
                    rel[c] = S.n["t"]
            else:
                for c in range(n):
                    w = list(sq_waits[c]) if sq_waits else []
                    t_mm(pb[sb1][0:1, :], onesv[:], sqs[c],
                         start=(c == 0), stop=(c == n - 1),
                         waits=(bank_wait(sb1) if c == 0 else []) + w,
                         tag=f"{tag}sq")
            tcnt = S.n["t"]
            inv = 1.0 / (128 * n)
            v_op(lambda v: v.tensor_scalar(out=row_m[:], in0=pb[sb0][0:1, :],
                                           scalar1=inv, scalar2=None, op0=OP.mult),
                 waits=[("t", tcnt)], tag=f"{tag}m")
            v_op(lambda v: v.tensor_scalar(out=row_q[:], in0=pb[sb1][0:1, :],
                                           scalar1=inv, scalar2=None, op0=OP.mult),
                 tag=f"{tag}q")
            bank_guard[sb0] = ("v", S.n["v"])
            bank_guard[sb1] = ("v", S.n["v"])
            v_op(lambda v: v.tensor_tensor(out=row_mr[:], in0=row_m[:],
                                           in1=row_m[:], op=OP.mult),
                 tag=f"{tag}m2")
            v_op(lambda v: v.scalar_tensor_tensor(out=row_q[:], in0=row_q[:],
                                                  scalar=1e-5, in1=row_mr[:],
                                                  op0=OP.add, op1=OP.subtract),
                 tag=f"{tag}var")
            vcnt = S.n["v"]
            a_op(lambda a: a.activation(out=row_r[:], in_=row_q[:], func=AF.Ln),
                 waits=[("v", vcnt)], tag=f"{tag}ln")
            a_op(lambda a: a.activation(out=row_r[:], in_=row_r[:], func=AF.Exp,
                                        scale=-0.5), tag=f"{tag}ex")
            acnt = S.n["a"]
            v_op(lambda v: v.tensor_tensor(out=row_mr[:], in0=row_m[:],
                                           in1=row_r[:], op=OP.mult),
                 waits=[("a", acnt)], tag=f"{tag}mr")
            vcnt = S.n["v"]
            t_mm(pb[bb0][:, :], c_ones1[:], row_r[:], start=True, stop=True,
                 waits=bank_wait(bb0) + [("v", vcnt)], tag=f"{tag}bcr")
            t_mm(pb[bb1][:, :], c_ones1[:], row_mr[:], start=True, stop=True,
                 waits=bank_wait(bb1), tag=f"{tag}bcmr")
            return S.n["t"]

        with ExitStack() as attn_ctx:
            asb = lambda name, shape, dtp: attn_ctx.enter_context(
                nc.sbuf_tensor(name, shape, dtp))
            c_diag = asb("c_diag", [128, NC_H, 4, 128], BF16)
            xhpre = asb("xhpre", [128, NC_H, 3 + ST], BF16)
            hcol = asb("hcol", [128, NC_H], F32)
            sxbf2 = asb("sxbf2", [128, 2, NC_D, ST], BF16)
            sxnbf = asb("sxnbf", [128, NC_D, ST], BF16)
            ssqbf2 = asb("ssqbf2", [128, 2, NC_D, ST], BF16)
            sgeg = asb("sgeg", [128, NC_H, ST], BF16)
            sxh = asb("sxh", [128, NC_H, ST], BF16)
            ssf = asb("ssf", [128, 2, ST], F32)
            ssi = asb("ssi", [128, 2, ST], F32)
            sa2 = asb("sa2", [128, 2, ST], F32)
            sao = asb("sao", [128, NC_D, ST], F32)
            wi = WStream([asb(f"wi{j}", [128, NC_D, 128], BF16) for j in range(2)], "wi", "wi")
            wg = WStream([asb(f"wg{j}", [128, NC_H, 128], BF16) for j in range(3)], "wg", "wg")
            wo = WStream([asb(f"wo{j}", [128, 4, 128], BF16) for j in range(4)], "wo", "wo")

            cw_cnt = g_dma(c_diag[:], d_diag, tag="diag")
            wc = cw_cnt
            a_op(lambda a: a.activation(out=c_cvec[:], in_=c_cvec[:], func=AF.Exp),
                 waits=[("c", wc)], tag="sp_exp")
            v_op(lambda v: v.tensor_scalar(out=c_cvec[:], in0=c_cvec[:], scalar1=1.0,
                                           scalar2=None, op0=OP.add),
                 waits=[("a", S.n["a"])], tag="sp_add")
            a_op(lambda a: a.activation(out=c_cvec[:], in_=c_cvec[:], func=AF.Ln),
                 waits=[("v", S.n["v"])], tag="sp_ln")
            v_op(lambda v: v.tensor_scalar(out=c_cvec[:], in0=c_cvec[:], scalar1=-8.0,
                                           scalar2=None, op0=OP.mult),
                 waits=[("a", S.n["a"])], tag="cvec")
            v_op(lambda v: v.memset(xhpre[:, :, 0:3], 0.0), tag="haloz")
            v_op(lambda v: v.memset(hcol[:], 0.0), tag="hcolz")

            guard = {}

            import os as _os
            _nst = int(_os.environ.get("KDBG_NST", NST))
            pre = {}
            pre[0] = {"xld": g_dma(sxbf2[:, 0], d_x[:, :, 0:ST], sem="x0", tag="x0"),
                      "xsem": "x0"}
            sq_v0 = []
            for c in range(NC_D):
                v_op(lambda v, c=c: v.tensor_tensor(out=ssqbf2[:, 0, c, :],
                                                    in0=sxbf2[:, 0, c, :],
                                                    in1=sxbf2[:, 0, c, :], op=OP.mult),
                     waits=([("x0", pre[0]["xld"])]) if c == 0 else [],
                     tag="sq0")
                sq_v0.append([("v", S.n["v"])])
            pre[0]["sq_v"] = sq_v0

            for st in range(_nst):
                main = st >= 4
                T0 = st * ST
                par = st % 2
                sxbf = sxbf2[:, par]
                ssqbf = ssqbf2[:, par]

                xld = pre[st]["xld"]
                sq_v = pre[st]["sq_v"]
                xw = [(pre[st]["xsem"], xld), ("c", wc)]
                stats_done = lngate([sxbf[:, c, :] for c in range(NC_D)],
                                    [ssqbf[:, c, :] for c in range(NC_D)],
                                    None, f"ln0_{st}",
                                    src_waits=[xw] + [[]] * (NC_D - 1),
                                    sq_waits=sq_v)
                xnw = [guard["sxnbf"]] if "sxnbf" in guard else []
                xn_v = {}
                for c in range(NC_D):
                    v_op(lambda v, c=c, sxbf=sxbf, ssqbf=ssqbf: v.tensor_tensor(
                            out=ssqbf[:, c, :], in0=sxbf[:, c, :],
                            in1=pb[6][:, :], op=OP.mult),
                         waits=([("t", stats_done)] if c == 0 else []),
                         tag=f"xna{st}")
                    v_op(lambda v, c=c, ssqbf=ssqbf: v.tensor_tensor(
                            out=sxnbf[:, c, :], in0=ssqbf[:, c, :],
                            in1=pb[7][:, :], op=OP.subtract),
                         waits=xnw if c == 0 else [], tag=f"xnb{st}")
                    xn_v[c] = ("v", S.n["v"])
                xn_done = ("v", S.n["v"])
                guard[("ssqbf", par)] = xn_done
                if not main:
                    guard[("sxbf", par)] = xn_done
                bank_guard[6] = xn_done
                bank_guard[7] = xn_done
                if st + 1 < _nst:
                    np_ = (st + 1) % 2
                    lw = [g for g in (guard.get(("sxbf", np_)),) if g]
                    nxld = g_dma(sxbf2[:, np_], d_x[:, :, T0 + ST:T0 + 2 * ST],
                                 waits=lw, sem=f"x{np_}", tag=f"x{st + 1}")
                    sqw = [g for g in (guard.get(("ssqbf", np_)),) if g]
                    nsq = []
                    for c in range(NC_D):
                        v_op(lambda v, c=c, np_=np_: v.tensor_tensor(
                                out=ssqbf2[:, np_, c, :],
                                in0=sxbf2[:, np_, c, :],
                                in1=sxbf2[:, np_, c, :], op=OP.mult),
                             waits=([(f"x{np_}", nxld)] + sqw) if c == 0 else [],
                             tag=f"sq{st + 1}")
                        nsq.append([("v", S.n["v"])])
                    pre[st + 1] = {"xld": nxld, "sq_v": nsq, "xsem": f"x{np_}"}

                if st > 0:
                    rollw = [guard["xhpre_roll"]] if "xhpre_roll" in guard else []
                    v_op(lambda v: v.tensor_copy(out=xhpre[:, :, 0:3],
                                                 in_=xhpre[:, :, ST:ST + 3]),
                         waits=rollw, tag=f"roll{st}")
                roll_done = ("v", S.n["v"])

                halves = [(d_wi_x, "xh")] + ([(d_wi_g, "gate")] if main else [])
                for src, kind in halves:
                    for m in range(NC_H):
                        li = wi.load(src[m])
                        b = m % 2
                        for k in range(NC_D):
                            t_mm(pb[b][:, :], wi.slot(li)[:, k, :], sxnbf[:, k, :],
                                 start=(k == 0), stop=(k == NC_D - 1),
                                 waits=((bank_wait(b) + [wi.wait(li)]) if k == 0 else [])
                                       + ([xn_v[k]] if m == 0 and kind == halves[0][1] else []),
                                 tag=f"wi{kind}{st}")
                        wi.done(li)
                        tcnt = S.n["t"]
                        if kind == "xh":
                            v_op(lambda v, m=m, b=b: v.tensor_copy(
                                    out=xhpre[:, m, 3:3 + ST], in_=pb[b][:, :]),
                                 waits=[("t", tcnt), roll_done], tag=f"evxh{st}")
                            bank_guard[b] = ("v", S.n["v"])
                        else:
                            gw_ = [guard["sgeg"]] if (m == 0 and "sgeg" in guard) else []
                            import os as _os3
                            _gf = AF.Sigmoid if _os3.environ.get("KDBG_NOGELU") else AF.Gelu
                            a_op(lambda a, m=m, b=b, _gf=_gf: a.activation(
                                    out=sgeg[:, m, :], in_=pb[b][:, :], func=_gf),
                                 waits=[("t", tcnt)] + gw_, tag=f"evgeg{st}")
                            bank_guard[b] = ("a", S.n["a"])
                dw = [("c", cw_cnt)]
                xh_a = {}
                xhw = [g for k in ("sxh_t", "sxh_v") if (g := guard.get(k))]
                for c in range(NC_H):
                    b = c % 2
                    for dtap in range(4):
                        t_mm(pb[b][:, :], c_diag[:, c, dtap, :],
                             xhpre[:, c, 3 - dtap:3 - dtap + ST],
                             start=(dtap == 0), stop=(dtap == 3),
                             waits=(bank_wait(b) + (dw if c == 0 else [])),
                             tag=f"conv{st}")
                    tcnt = S.n["t"]
                    a_op(lambda a, c=c, b=b: a.activation(
                            out=sxh[:, c, :], in_=pb[b][:, :], func=AF.Identity,
                            bias=c_cb[:, c:c + 1], scale=1.0),
                         waits=[("t", tcnt)] + (xhw if c == 0 else []),
                         tag=f"evconv{st}")
                    bank_guard[b] = ("a", S.n["a"])
                    xh_a[c] = ("a", S.n["a"])
                guard["xhpre_roll"] = ("t", S.n["t"])
                xh_ready = ("a", S.n["a"])

                geg_a = S.n["a"]
                if st == 4:
                    v_op(lambda v: v.tensor_scalar(out=hcol[:], in0=hcol[:],
                                                   scalar1=c_mask[:], scalar2=None,
                                                   op0=OP.mult),
                         waits=[("c", wc)], tag="maskcarry")
                def midchain(mi, main):
                    pp = mi % 2
                    sig_done = ("a", sig_cnt[mi])
                    v_op(lambda v, pp=pp, mi=mi: v.tensor_scalar(
                            out=ssf[:, pp, :], in0=ssf[:, pp, :],
                            scalar1=c_cvec[:, mi:mi + 1], scalar2=None, op0=OP.mult),
                         waits=[sig_done], tag=f"la{st}")
                    vc = S.n["v"]
                    a_op(lambda a, pp=pp: a.activation(
                            out=ssf[:, pp, :], in_=ssf[:, pp, :], func=AF.Exp),
                         waits=[("v", vc)], tag=f"alpha{st}")
                    al_done = ("a", S.n["a"])
                    v_op(lambda v, pp=pp: v.tensor_tensor(
                            out=sa2[:, pp, :], in0=ssf[:, pp, :], in1=ssf[:, pp, :],
                            op=OP.mult), waits=[al_done], tag=f"a2{st}")
                    v_op(lambda v, pp=pp: v.tensor_scalar(
                            out=sa2[:, pp, :], in0=sa2[:, pp, :], scalar1=-1.0,
                            scalar2=1.0 + 1e-6, op0=OP.mult, op1=OP.add),
                         tag=f"y{st}")
                    return S.n["v"]

                def midtail(mi, main):
                    pp = mi % 2
                    beta_done = ("a", beta_cnt[mi])
                    v_op(lambda v, pp=pp: v.tensor_tensor(
                            out=ssi[:, pp, :], in0=sa2[:, pp, :], in1=ssi[:, pp, :],
                            op=OP.mult), waits=[beta_done], tag=f"bsi{st}")
                    v_op(lambda v, pp=pp, mi=mi: v.tensor_tensor(
                            out=ssi[:, pp, :], in0=ssi[:, pp, :], in1=sxh[:, mi, :],
                            op=OP.mult), tag=f"xs{st}")
                    v_op(lambda v, pp=pp, mi=mi: v.tensor_tensor_scan(
                            out=sa2[:, pp, :], data0=ssf[:, pp, :], data1=ssi[:, pp, :],
                            initial=hcol[:, mi:mi + 1], op0=OP.mult, op1=OP.add),
                         tag=f"scan{st}")
                    v_op(lambda v, pp=pp, mi=mi: v.tensor_copy(
                            out=hcol[:, mi:mi + 1], in_=sa2[:, pp, ST - 1:ST]),
                         tag=f"hc{st}")
                    if main:
                        v_op(lambda v, pp=pp, mi=mi: v.tensor_tensor(
                                out=sgeg[:, mi, :], in0=sgeg[:, mi, :],
                                in1=sa2[:, pp, :], op=OP.mult),
                             waits=[("a", geg_a)], tag=f"gh{st}")
                        gh_v[mi] = ("v", S.n["v"])
                    guard[("mid", pp)] = ("v", S.n["v"])

                sig_cnt = {}
                beta_cnt = {}
                gh_v = {}
                for mi in range(NC_H):
                    pp = mi % 2
                    for sub in range(2):
                        morig = mi if sub == 0 else NC_H + mi
                        li = wg.load(d_wg[morig])
                        b = sub
                        for k in range(NC_H):
                            t_mm(pb[b][:, :], wg.slot(li)[:, k, :], sxh[:, k, :],
                                 start=(k == 0), stop=(k == NC_H - 1),
                                 waits=((bank_wait(b) + [wg.wait(li)]) if k == 0 else [])
                                       + ([xh_a[k]] if mi == 0 and sub == 0 else []),
                                 tag=f"wg{st}")
                        wg.done(li)
                        tcnt = S.n["t"]
                        dst = ssf if sub == 0 else ssi
                        dguard = guard.get(("mid", pp))
                        a_op(lambda a, dst=dst, pp=pp, b=b, morig=morig: a.activation(
                                out=dst[:, pp, :], in_=pb[b][:, :], func=AF.Sigmoid,
                                bias=c_bg[:, morig:morig + 1], scale=1.0),
                             waits=[("t", tcnt)] + ([dguard] if dguard else []),
                             tag=f"evg{st}")
                        bank_guard[b] = ("a", S.n["a"])
                    sig_cnt[mi] = S.n["a"]
                    if pp == 1:
                        yv = {}
                        for mj in (mi - 1, mi):
                            yv[mj] = midchain(mj, main)
                        for mj in (mi - 1, mi):
                            a_op(lambda a, ppj=mj % 2: a.activation(
                                    out=sa2[:, ppj, :], in_=sa2[:, ppj, :], func=AF.Ln),
                                 waits=[("v", yv[mj])], tag=f"lny{st}")
                        for mj in (mi - 1, mi):
                            a_op(lambda a, ppj=mj % 2: a.activation(
                                    out=sa2[:, ppj, :], in_=sa2[:, ppj, :], func=AF.Exp,
                                    scale=0.5), tag=f"beta{st}")
                            beta_cnt[mj] = S.n["a"]
                        for mj in (mi - 1, mi):
                            midtail(mj, main)
                guard["sxh_t"] = ("t", S.n["t"])
                guard["sxh_v"] = ("v", S.n["v"])

                if not main:
                    continue
                import os as _os2
                if _os2.environ.get("KDBG_NOWOUT"):
                    continue

                for pss in range(2):
                    lis = {kk: wo.load(d_wo2[pss * NC_H + kk]) for kk in range(4)}
                    for k in range(NC_H):
                        for mm4 in range(4):
                            t_mm(pb[mm4][:, :], wo.slot(lis[k])[:, mm4, :],
                                 sgeg[:, k, :],
                                 start=(k == 0), stop=(k == NC_H - 1),
                                 force_inc=(mm4 == 3),
                                 waits=(bank_wait(mm4) if k == 0 else [])
                                       + ([wo.wait(lis[k]), gh_v[k]] if mm4 == 0 else []),
                                 tag=f"wo{st}")
                        wo.done(lis[k])
                        if k + 4 < NC_H:
                            lis[k + 4] = wo.load(d_wo2[pss * NC_H + k + 4])
                    for mm4 in range(4):
                        m = pss * 4 + mm4
                        b = mm4
                        tcnt = S.n["t"]
                        guard["sgeg"] = ("t", tcnt)
                        if _os2.environ.get("KDBG_NOEVAO"):
                            bank_guard[b] = ("t", tcnt)
                            continue
                        aow = [guard["sao"]] if (m == 0 and "sao" in guard) else []
                        a_op(lambda a, m=m, b=b: a.copy(out=sao[:, m, :], in_=pb[b][:, :]),
                             waits=[("t", tcnt)] + aow, tag=f"evao{st}")
                        a_op(lambda a, m=m, b=b: a.copy(out=sxnbf[:, m, :],
                                                        in_=pb[b][:, :]),
                             tag=f"aobf{st}")
                        ao_a = ("a", S.n["a"])
                        v_op(lambda v, m=m, b=b, ssqbf=ssqbf: v.tensor_tensor(
                                out=ssqbf[:, m, :], in0=sxnbf[:, m, :],
                                in1=sxnbf[:, m, :], op=OP.mult),
                             waits=[ao_a], tag=f"aosq{st}")
                        bank_guard[b] = [ao_a]
                stats_done = lngate([sxnbf[:, c, :] for c in range(NC_D)],
                                    [ssqbf[:, c, :] for c in range(NC_D)],
                                    ("v", S.n["v"]), f"ln1_{st}", banks=(2, 3, 2, 3))
                guard["sxnbf"] = ("t", stats_done)
                guard[("ssqbf", par)] = ("t", stats_done)
                Tm = T0 - NT
                for c in range(NC_D):
                    v_op(lambda v, c=c: v.tensor_tensor(
                            out=sao[:, c, :], in0=sao[:, c, :], in1=pb[2][:, :],
                            op=OP.mult),
                         waits=([("t", stats_done)] if c == 0 else []), tag=f"x1a{st}")
                    v_op(lambda v, c=c: v.tensor_tensor(
                            out=sao[:, c, :], in0=sao[:, c, :], in1=pb[3][:, :],
                            op=OP.subtract), tag=f"x1b{st}")
                    v_op(lambda v, c=c, Tm=Tm, sxbf=sxbf: v.tensor_tensor(
                            out=x1[:, c, Tm:Tm + ST], in0=sao[:, c, :],
                            in1=sxbf[:, c, :], op=OP.add), tag=f"x1c{st}")
                guard["sao"] = ("v", S.n["v"])
                guard[("sxbf", par)] = ("v", S.n["v"])
                bank_guard[2] = ("v", S.n["v"])
                bank_guard[3] = ("v", S.n["v"])

            attn_v = S.n["v"]
            attn_a = S.n["a"]
            attn_t = S.n["t"]

        if STAGE == "attn":
            st_cnt = None
            for q in range(4):
                st_cnt = s_dma_out(d_out[:, :, q * ST:(q + 1) * ST],
                                   x1[:, :, q * ST:(q + 1) * ST],
                                   waits=[("v", attn_v)], tag="outx1")
            S.emit("sync", lambda s: None, waits=[("o", st_cnt)], tag="fin")
        else:
            with ExitStack() as ffn_ctx:
                fsb = lambda name, shape, dtp: ffn_ctx.enter_context(
                    nc.sbuf_tensor(name, shape, dtp))
                xn2 = fsb("xn2", [128, NC_D, 2 * ST], BF16)
                racc = fsb("racc", [128, NC_D, 2 * ST], F32)
                a_s = fsb("a_s", [128, E, 2 * ST], BF16)
                xbf2 = lambda c: a_s[:, c, 0:ST]
                sq2 = lambda c: a_s[:, c, ST:2 * ST]
                bufA = fsb("bufA", [128, 2, ST], F32)
                bufB = fsb("bufB", [128, 2, ST], F32)
                bufC = fsb("bufC", [128, 2, ST], F32)
                slgf = fsb("slgf", [8, 2 * ST], F32)
                sltok = fsb("sltok", [128, 8, E], F32)
                scwtok = fsb("scwtok", [128, 8, E], F32)
                scwf = fsb("scwf", [8, 2 * ST], F32)
                scwfbf = fsb("scwfbf", [8, 2 * ST], BF16)
                sebuf = fsb("sebuf", [128, E], F32)
                sscore = fsb("sscore", [128, E], F32)
                sbias = fsb("sbias", [128, E], F32)
                smsk1 = fsb("smsk1", [128, E], F32)
                smsk2 = fsb("smsk2", [128, E], F32)
                r1 = fsb("r1", [128, 1], F32)
                r2 = fsb("r2", [128, 1], F32)
                outtile = fsb("outt", [128, NC_D, ST], F32)
                wq = [WStream([fsb(f"wq{q_}{j}", [128, NC_D, 128], BF16)
                               for j in range(2)], f"wq{q_}", f"q{q_}")
                      for q_ in range(4)]
                w2s = WStream([fsb(f"w2s{j}", [128, E, 128], BF16)
                               for j in range(3)], "w2s", "w2")
                c_sel8 = fsb("c_sel8", [8, E, 128], F32)
                c_gw = fsb("c_gw", [128, NC_D, E], BF16)
                c_b2t = fsb("c_b2t", [8, E, 128], BF16)

                bar = [("t", attn_t), ("v", attn_v), ("a", attn_a)]
                for _e in ("tensor", "vector", "scalar", "gpsimd", "sync"):
                    S.emit(_e, lambda e: e.nop(), waits=bar, tag="ffnbar")
                g_dma(c_sel8[:], d_sel8, waits=bar, sem="c2", tag="selc")
                g_dma(c_gw[:], d_gw, waits=[], sem="c2", tag="gwc")
                wc2 = g_dma(c_b2t[:], d_b2t, waits=[], sem="c2", tag="b2c")

                import os as _os
                guard = {}
                store_cnt = 0
                for p in range(2):
                    Tp = p * 2 * ST
                    xn2_v = {}
                    for t in range(2):
                        tsl = slice(Tp + t * ST, Tp + (t + 1) * ST)
                        bg = [guard["bufC"]] if "bufC" in guard else []

                        def sqg2(c, gw, tsl=tsl, bg=bg):
                            a_op(lambda a, c=c, tsl=tsl: a.activation(
                                    out=bufC[:, c % 2, :], in_=x1[:, c, tsl],
                                    func=AF.Square),
                                 waits=gw + (bg if c == 0 else []), tag=f"l2q{p}{t}")
                            return bufC[:, c % 2, :]

                        sd = lngate([x1[:, c, tsl] for c in range(NC_D)],
                                    None, None, f"ln2_{p}{t}",
                                    src_waits=[[bar[1]]] + [[]] * (NC_D - 1),
                                    fp32=True, sq_gen=sqg2)
                        guard["bufC"] = ("t", sd)
                        for c in range(NC_D):
                            v_op(lambda v, c=c, tsl=tsl: v.tensor_tensor(
                                    out=bufA[:, c % 2, :], in0=x1[:, c, tsl],
                                    in1=pb[6][:, :], op=OP.mult),
                                 waits=([("t", sd)] if c == 0 else []),
                                 tag=f"xn2a{p}{t}")
                            v_op(lambda v, c=c, t=t: v.tensor_tensor(
                                    out=xn2[:, c, t * ST:(t + 1) * ST],
                                    in0=bufA[:, c % 2, :],
                                    in1=pb[7][:, :], op=OP.subtract),
                                 tag=f"xn2b{p}{t}")
                            xn2_v[(t, c)] = ("v", S.n["v"])
                        guard["bufA"] = ("v", S.n["v"])

                        bank_guard[6] = ("v", S.n["v"])
                        bank_guard[7] = ("v", S.n["v"])
                    xn2_done = ("v", S.n["v"])

                    cw_blk_v = {}
                    for t in range(2):
                        tloc = slice(t * ST, (t + 1) * ST)
                        for k in range(NC_D):
                            t_mm(pb[4][0:8, :], c_gw[:, k, :], xn2[:, k, tloc],
                                 start=(k == 0), stop=(k == NC_D - 1),
                                 waits=((bank_wait(4) + [("c2", wc2)]) if k == 0 else [])
                                       + [xn2_v[(t, k)]],
                                 tag=f"lg{p}{t}")
                        tc = S.n["t"]
                        v_op(lambda v, t=t, tloc=tloc: v.tensor_copy(
                                out=slgf[0:8, tloc], in_=pb[4][0:8, :]),
                             waits=[("t", tc)], tag=f"lgev{p}{t}")
                        bank_guard[4] = ("v", S.n["v"])
                        lg_done = ("v", S.n["v"])
                        for blk in range(4):
                            t_mm(pb[5][:, blk * 8:(blk + 1) * 8],
                                 slgf[0:8, t * ST + blk * 128: t * ST + (blk + 1) * 128],
                                 c_id8[:], start=True, stop=True,
                                 waits=(bank_wait(5) + [lg_done]), is_tp=True,
                                 tag=f"lgtp{p}{t}")
                        tc = S.n["t"]
                        v_op(lambda v, t=t: v.tensor_copy(
                                out=sltok[:, 4 * t:4 * t + 4, :],
                                in_=pb[5][:, 0:32]),
                             waits=[("t", tc)], tag=f"ltok{p}{t}")
                        bank_guard[5] = ("v", S.n["v"])
                        for blk in range(4):
                            bi = 4 * t + blk
                            v_op(lambda v, bi=bi: v.tensor_reduce(
                                    out=r1[:], in_=sltok[:, bi, :], axis=AX.X,
                                    op=OP.max), tag=f"rmax{p}{bi}")
                            v_op(lambda v: v.tensor_scalar(
                                    out=r2[:], in0=r1[:], scalar1=-1.0,
                                    scalar2=None, op0=OP.mult), tag=f"rng{p}{bi}")
                            vc = S.n["v"]
                            a_op(lambda a, bi=bi: a.activation(
                                    out=sebuf[:], in_=sltok[:, bi, :], func=AF.Exp,
                                    bias=r2[:], scale=1.0),
                                 waits=[("v", vc)], tag=f"rexp{p}{bi}")
                            ac = S.n["a"]
                            v_op(lambda v: v.tensor_reduce(
                                    out=r1[:], in_=sebuf[:], axis=AX.X, op=OP.add),
                                 waits=[("a", ac)], tag=f"rsum{p}{bi}")
                            v_op(lambda v: v.reciprocal(out=r1[:], in_=r1[:]),
                                 tag=f"rrec{p}{bi}")
                            v_op(lambda v: v.tensor_scalar(
                                    out=sscore[:], in0=sebuf[:], scalar1=r1[:],
                                    scalar2=None, op0=OP.mult), tag=f"rsc{p}{bi}")
                            v_op(lambda v: v.tensor_tensor(
                                    out=sbias[:], in0=sscore[:], in1=c_gbb[:],
                                    op=OP.add), tag=f"rb{p}{bi}")
                            v_op(lambda v: v.tensor_reduce(
                                    out=r1[:], in_=sbias[:], axis=AX.X,
                                    op=OP.max), tag=f"rm1{p}{bi}")
                            v_op(lambda v: v.tensor_scalar(
                                    out=smsk1[:], in0=sbias[:], scalar1=r1[:],
                                    scalar2=None, op0=OP.is_equal),
                                 tag=f"rq1{p}{bi}")
                            v_op(lambda v: v.scalar_tensor_tensor(
                                    out=smsk2[:], in0=smsk1[:], scalar=-1e9,
                                    in1=sbias[:], op0=OP.mult, op1=OP.add),
                                 tag=f"rb2{p}{bi}")
                            v_op(lambda v: v.tensor_reduce(
                                    out=r1[:], in_=smsk2[:], axis=AX.X,
                                    op=OP.max), tag=f"rm2{p}{bi}")
                            v_op(lambda v: v.tensor_scalar(
                                    out=smsk2[:], in0=smsk2[:], scalar1=r1[:],
                                    scalar2=None, op0=OP.is_equal),
                                 tag=f"rq2{p}{bi}")
                            v_op(lambda v: v.tensor_tensor(
                                    out=smsk1[:], in0=smsk1[:], in1=smsk2[:],
                                    op=OP.add), tag=f"rq12{p}{bi}")
                            v_op(lambda v, bi=bi: v.tensor_tensor(
                                    out=scwtok[:, bi, :], in0=smsk1[:],
                                    in1=sscore[:], op=OP.mult), tag=f"rcw{p}{bi}")
                            cw_blk_v[bi] = S.n["v"]
                        for blk in range(4):
                            t_mm(pb[5][0:8, blk * 128:(blk + 1) * 128],
                                 scwtok[:, 4 * t + blk, :], c_id128[:],
                                 start=True, stop=True,
                                 waits=(bank_wait(5) if blk == 0 else [])
                                       + [("v", cw_blk_v[4 * t + blk])],
                                 is_tp=True, tag=f"cwtp{p}{t}")
                        tc = S.n["t"]
                        v_op(lambda v, t=t: v.tensor_copy(
                                out=scwf[0:8, t * ST:(t + 1) * ST],
                                in_=pb[5][0:8, 0:512]),
                             waits=[("t", tc)], tag=f"cwf{p}{t}")
                        v_op(lambda v, t=t: v.tensor_copy(
                                out=scwfbf[0:8, t * ST:(t + 1) * ST],
                                in_=scwf[0:8, t * ST:(t + 1) * ST]),
                             tag=f"cwfb{p}{t}")
                        bank_guard[5] = ("v", S.n["v"])
                    cw_done = ("v", S.n["v"])

                    racc_v = {}
                    for e in range(9):
                        shared = (e == 8)
                        if not shared:
                            for t in range(2):
                                t_mm(pb[6 + t][:, :], c_sel8[0:8, e, :],
                                     scwf[0:8, t * ST:(t + 1) * ST],
                                     start=True, stop=True,
                                     waits=(bank_wait(6 + t)
                                            + ([cw_done, ("c2", wc2)] if t == 0 else [])),
                                     tag=f"cwb{p}{e}")
                        cwb_t = S.n["t"]
                        for m in range(E):
                            ql = []
                            for q_ in range(4):
                                srcs = ([d_w1g, d_w3g, d_w1l, d_w3l][q_][e, m]
                                        if not shared else
                                        [d_sw1g, d_sw3g, d_sw1l, d_sw3l][q_][m])
                                ql.append(wq[q_].load(srcs))
                            for t in range(2):
                                tloc = slice(t * ST, (t + 1) * ST)
                                pp = (m * 2 + t) % 2
                                for q_ in range(4):
                                    for k in range(NC_D):
                                        t_mm(pb[q_][:, :], wq[q_].slot(ql[q_])[:, k, :],
                                             xn2[:, k, tloc],
                                             start=(k == 0), stop=(k == NC_D - 1),
                                             waits=(bank_wait(q_)
                                                    + ([wq[q_].wait(ql[q_])] if k == 0 else [])
                                                    + ([xn2_v[(t, k)]] if (e == 0 and m == 0 and q_ == 0) else [])
                                                    + ([("t", cwb_t)] if (k == 0 and q_ == 0 and t == 0) else [])),
                                             tag=f"q{q_}{p}{e}{m}{t}")
                                tc = S.n["t"]
                                bias = lambda cb, sb2_: (
                                    sb2_[:, m:m + 1] if shared else cb[:, e, m:m + 1])
                                a_op(lambda a, pp=pp, b_=bias(c_b1g, c_sb1g): a.activation(
                                        out=bufA[:, pp, :], in_=pb[0][:, :],
                                        func=AF.Identity, bias=b_, scale=1.0),
                                     waits=[("t", tc)], tag=f"t1{p}{e}{m}{t}")
                                a1 = S.n["a"]
                                v_op(lambda v, pp=pp, b_=bias(c_b3g, c_sb3g): v.scalar_tensor_tensor(
                                        out=bufA[:, pp, :], in0=pb[1][:, :],
                                        scalar=b_, in1=bufA[:, pp, :],
                                        op0=OP.add, op1=OP.mult),
                                     waits=[("t", tc), ("a", a1)], tag=f"gg{p}{e}{m}{t}")
                                a_op(lambda a, pp=pp, b_=bias(c_b1l, c_sb1l): a.activation(
                                        out=bufB[:, pp, :], in_=pb[2][:, :],
                                        func=AF.Identity, bias=b_, scale=1.0),
                                     waits=[("t", tc)], tag=f"t2{p}{e}{m}{t}")
                                a2 = S.n["a"]
                                v_op(lambda v, pp=pp, b_=bias(c_b3l, c_sb3l): v.scalar_tensor_tensor(
                                        out=bufB[:, pp, :], in0=pb[3][:, :],
                                        scalar=b_, in1=bufB[:, pp, :],
                                        op0=OP.add, op1=OP.mult),
                                     waits=[("a", a2)], tag=f"ll{p}{e}{m}{t}")
                                bank_guard[0] = ("a", a1)
                                bank_guard[1] = ("v", S.n["v"] - 1)
                                bank_guard[2] = ("a", a2)
                                bank_guard[3] = ("v", S.n["v"])
                                v_op(lambda v, pp=pp: v.tensor_scalar(
                                        out=bufA[:, pp, :], in0=bufA[:, pp, :],
                                        scalar1=7.0, scalar2=None, op0=OP.min),
                                     tag=f"g7{p}{e}{m}{t}")
                                vc = S.n["v"]
                                a_op(lambda a, pp=pp: a.activation(
                                        out=bufC[:, pp, :], in_=bufA[:, pp, :],
                                        func=AF.Sigmoid, scale=1.702),
                                     waits=[("v", vc)], tag=f"sg{p}{e}{m}{t}")
                                a3 = S.n["a"]
                                v_op(lambda v, pp=pp: v.tensor_tensor(
                                        out=bufA[:, pp, :], in0=bufA[:, pp, :],
                                        in1=bufC[:, pp, :], op=OP.mult),
                                     waits=[("a", a3)], tag=f"p1{p}{e}{m}{t}")
                                v_op(lambda v, pp=pp: v.tensor_scalar(
                                        out=bufB[:, pp, :], in0=bufB[:, pp, :],
                                        scalar1=-7.0, scalar2=7.0,
                                        op0=OP.max, op1=OP.min),
                                     tag=f"lc{p}{e}{m}{t}")
                                if shared:
                                    v_op(lambda v, pp=pp: v.tensor_scalar(
                                            out=bufB[:, pp, :], in0=bufB[:, pp, :],
                                            scalar1=1.0, scalar2=None, op0=OP.add),
                                         tag=f"p2{p}{e}{m}{t}")
                                else:
                                    v_op(lambda v, pp=pp, t=t: v.scalar_tensor_tensor(
                                            out=bufB[:, pp, :], in0=bufB[:, pp, :],
                                            scalar=1.0, in1=pb[6 + t][:, :],
                                            op0=OP.add, op1=OP.mult),
                                         tag=f"p2{p}{e}{m}{t}")
                                asg = [guard["a_s"]] if (e == 0 and m == 0 and t == 0 and "a_s" in guard) else []
                                v_op(lambda v, pp=pp, m=m, tloc=tloc: v.tensor_tensor(
                                        out=a_s[:, m, tloc], in0=bufA[:, pp, :],
                                        in1=bufB[:, pp, :], op=OP.mult),
                                     waits=asg, tag=f"as{p}{e}{m}{t}")
                            for q_ in range(4):
                                wq[q_].done(ql[q_])
                        as_done = ("v", S.n["v"])
                        if not shared:
                            bank_guard[6] = as_done
                            bank_guard[7] = as_done
                        for m2 in range(E):
                            li = w2s.load(d_w2[e, m2] if not shared else d_sw2[m2])
                            for t in range(2):
                                tloc = slice(t * ST, (t + 1) * ST)
                                b = 4 + t
                                for k in range(E):
                                    t_mm(pb[b][:, :], w2s.slot(li)[:, k, :],
                                         a_s[:, k, tloc],
                                         start=(k == 0),
                                         stop=(k == E - 1 and e != 0),
                                         waits=(bank_wait(b)
                                                + ([w2s.wait(li), as_done] if k == 0 else [])),
                                         tag=f"w2{p}{e}{m2}{t}")
                                if e == 0:
                                    t_mm(pb[b][:, :], c_b2t[0:8, m2, :],
                                         scwfbf[0:8, tloc],
                                         start=False, stop=True,
                                         waits=[], tag=f"b2{p}{m2}{t}")
                                tc = S.n["t"]
                                if e == 0:
                                    v_op(lambda v, m2=m2, tloc=tloc, b=b: v.tensor_copy(
                                            out=racc[:, m2, tloc], in_=pb[b][:, :]),
                                         waits=[("t", tc)], tag=f"ra{p}{e}{m2}{t}")
                                elif not shared:
                                    v_op(lambda v, m2=m2, tloc=tloc, b=b: v.tensor_tensor(
                                            out=racc[:, m2, tloc],
                                            in0=racc[:, m2, tloc], in1=pb[b][:, :],
                                            op=OP.add),
                                         waits=[("t", tc)], tag=f"ra{p}{e}{m2}{t}")
                                else:
                                    v_op(lambda v, m2=m2, tloc=tloc, b=b: v.scalar_tensor_tensor(
                                            out=racc[:, m2, tloc], in0=pb[b][:, :],
                                            scalar=c_sb2[:, m2:m2 + 1],
                                            in1=racc[:, m2, tloc],
                                            op0=OP.add, op1=OP.add),
                                         waits=[("t", tc)], tag=f"ra{p}{e}{m2}{t}")
                                bank_guard[b] = ("v", S.n["v"])
                                racc_v[(t, m2)] = S.n["v"]
                            w2s.done(li)
                    guard["a_s_w2"] = ("t", S.n["t"])
                    racc_done = ("v", S.n["v"])

                    if _os.environ.get("KDBG_NOLN3"):
                        if p == 1:
                            store_cnt = s_dma_out(d_out[:, :, 0:ST], racc[:, :, 0:ST],
                                                  waits=[racc_done], tag="dbgout")
                        continue
                    for t in range(2):
                        tsl = slice(Tp + t * ST, Tp + (t + 1) * ST)
                        tloc = slice(t * ST, (t + 1) * ST)
                        bg = [guard["bufC"]] if "bufC" in guard else []

                        def sqg3(c, gw, tloc=tloc, bg=bg):
                            a_op(lambda a, c=c, tloc=tloc: a.activation(
                                    out=bufC[:, c % 2, :], in_=racc[:, c, tloc],
                                    func=AF.Square),
                                 waits=gw + (bg if c == 0 else [])
                                       + [("v", racc_v[(t, c)])], tag=f"l3q{p}{t}")
                            return bufC[:, c % 2, :]

                        sd = lngate([racc[:, c, tloc] for c in range(NC_D)],
                                    None, None, f"ln3_{p}{t}",
                                    src_waits=[[("v", racc_v[(t, c)])] for c in range(NC_D)],
                                    banks=(0, 1, 0, 1), fp32=True, sq_gen=sqg3)
                        guard["bufC"] = ("t", sd)
                        og = [guard["outtile"]] if "outtile" in guard else []
                        for c in range(NC_D):
                            v_op(lambda v, c=c, tloc=tloc: v.tensor_tensor(
                                    out=bufA[:, 0, :],
                                    in0=racc[:, c, tloc], in1=pb[0][:, :],
                                    op=OP.mult),
                                 waits=([("t", sd)] if c == 0 else []),
                                 tag=f"o1{p}{t}")
                            v_op(lambda v, c=c: v.tensor_tensor(
                                    out=bufA[:, 0, :], in0=bufA[:, 0, :],
                                    in1=pb[1][:, :], op=OP.subtract),
                                 tag=f"o2{p}{t}")
                            v_op(lambda v, c=c, tsl=tsl: v.tensor_tensor(
                                    out=outtile[:, c, :], in0=bufA[:, 0, :],
                                    in1=x1[:, c, tsl], op=OP.add),
                                 waits=(og if c == 0 else []), tag=f"o3{p}{t}")
                        bank_guard[0] = ("v", S.n["v"])
                        bank_guard[1] = ("v", S.n["v"])
                        ot = S.n["v"]
                        store_cnt = s_dma_out(d_out[:, :, tsl], outtile[:, :, :],
                                              waits=[("v", ot)], tag=f"out{p}{t}")
                        guard["outtile"] = ("o", store_cnt)
                S.emit("sync", lambda s: None, waits=[("o", store_cnt)], tag="fin")

        with ExitStack() as semctx:
            sems = {name: semctx.enter_context(nc.semaphore(f"s_{name}"))
                    for name in S.n}
            block = semctx.enter_context(nc.Block())

            @block.sync
            def _(sync):
                S.replay("sync", sync, sems)

            @block.tensor
            def _(tensor):
                S.replay("tensor", tensor, sems)

            @block.vector
            def _(vector):
                S.replay("vector", vector, sems)

            @block.scalar
            def _(scalar):
                S.replay("scalar", scalar, sems)

            @block.gpsimd
            def _(gpsimd):
                S.replay("gpsimd", gpsimd, sems)

    return nc



def _prep_shared(i):
    W_in = i["L0_attn_W_in"]
    w_in_gate = np.ascontiguousarray(
        W_in[:H].reshape(NC_H, 128, NC_D, 128).transpose(0, 3, 2, 1))
    w_in_xh = np.ascontiguousarray(
        W_in[H:].reshape(NC_H, 128, NC_D, 128).transpose(0, 3, 2, 1))
    cw = i["L0_attn_conv_w"]
    diag = np.zeros((128, NC_H, 4, 128), np.float32)
    kk = np.arange(128)
    for c in range(NC_H):
        for d in range(4):
            diag[kk, c, d, kk] = cw[c * 128:(c + 1) * 128, 3 - d]
    wg = np.ascontiguousarray(
        i["L0_attn_W_gates"].reshape(NC_G, 128, NC_H, 128).transpose(0, 3, 2, 1))
    pair = [v for mi in range(NC_H) for v in (mi, NC_H + mi)]
    wo = np.ascontiguousarray(
        i["L0_attn_W_out"].reshape(2, 4, 128, NC_H, 128)
        .transpose(0, 3, 4, 1, 2).reshape(2 * NC_H, 128, 4, 128))
    fm = lambda v, n: np.ascontiguousarray(v.reshape(n, 128).T)
    quad = lambda w, sl: np.ascontiguousarray(
        w[:, sl, :].reshape(E, E, 128, NC_D, 128).transpose(0, 1, 4, 3, 2))
    squad = lambda w, sl: np.ascontiguousarray(
        w[sl, :].reshape(E, 128, NC_D, 128).transpose(0, 3, 2, 1))
    b_em = lambda b, sl: np.ascontiguousarray(
        b[:, sl].reshape(E, E, 128).transpose(2, 0, 1))
    d = {
        "wi_gate": w_in_gate, "wi_xh": w_in_xh, "convdiag": diag,
        "wgates": wg, "wout2": wo,
        "fbase": fm(i["L0_attn_forget_base"], NC_H),
        "convb": fm(i["L0_attn_conv_b"], NC_H),
        "bgates": fm(i["L0_attn_b_gates"], NC_G),
        "ones1": np.ones((1, 128), np.float32),
        "onescol": np.ones((128, 1), np.float32),
        "onescolf": np.ones((128, 1), np.float32),
        "id128": np.eye(128, dtype=np.float32),
        "id8": np.eye(8, dtype=np.float32),
        "sel8": np.ascontiguousarray(
            np.eye(8, dtype=np.float32)[:, :, None]
            * np.ones((1, 1, 128), np.float32)),
        "gatew": np.ascontiguousarray(
            i["L0_ffn_gate_w"].reshape(E, NC_D, 128).transpose(2, 1, 0)),
        "gatebb": np.ascontiguousarray(
            np.broadcast_to(i["L0_ffn_gate_b"][None, :], (128, E))),
        "w1g": quad(i["L0_ffn_w1"], slice(0, None, 2)),
        "w1l": quad(i["L0_ffn_w1"], slice(1, None, 2)),
        "w3g": quad(i["L0_ffn_w3"], slice(0, None, 2)),
        "w3l": quad(i["L0_ffn_w3"], slice(1, None, 2)),
        "sw1g": squad(i["L0_ffn_sw1"], slice(0, None, 2)),
        "sw1l": squad(i["L0_ffn_sw1"], slice(1, None, 2)),
        "sw3g": squad(i["L0_ffn_sw3"], slice(0, None, 2)),
        "sw3l": squad(i["L0_ffn_sw3"], slice(1, None, 2)),
        "sw2": np.ascontiguousarray(
            i["L0_ffn_sw2"].reshape(NC_D, 128, E, 128).transpose(0, 3, 2, 1)),
        "b1g": b_em(i["L0_ffn_b1"], slice(0, None, 2)),
        "b1l": b_em(i["L0_ffn_b1"], slice(1, None, 2)),
        "b3g": b_em(i["L0_ffn_b3"], slice(0, None, 2)),
        "b3l": b_em(i["L0_ffn_b3"], slice(1, None, 2)),
        "b2t": np.ascontiguousarray(i["L0_ffn_b2"].reshape(E, NC_D, 128)),
        "sb1g": fm(i["L0_ffn_sb1"][0::2], E), "sb1l": fm(i["L0_ffn_sb1"][1::2], E),
        "sb3g": fm(i["L0_ffn_sb3"][0::2], E), "sb3l": fm(i["L0_ffn_sb3"][1::2], E),
        "sb2": fm(i["L0_ffn_sb2"], NC_D),
    }
    return d


def kernel(**inputs):
    x = np.asarray(inputs["L0_attn_x"])
    shared = _prep_shared(inputs)
    w2 = inputs["L0_ffn_w2"]
    shared["w2"] = np.ascontiguousarray(
        w2.reshape(E, NC_D, 128, E, 128).transpose(0, 1, 4, 3, 2))

    in_maps = []
    for core in range(8):
        seq, half = core // 2, core % 2
        mainx = x[seq, half * NT:(half + 1) * NT]
        prefx = (x[seq, 0:NT] if half == 1
                 else np.zeros((NT, D), np.float32))
        xin = np.concatenate([prefx, mainx], axis=0)
        xin_fm = np.ascontiguousarray(
            xin.T.reshape(NC_D, 128, 2 * NT).transpose(1, 0, 2))
        m = dict(shared)
        m["xin"] = xin_fm
        m["cmask"] = np.full((128, 1), float(half), np.float32)
        in_maps.append(m)

    nc = build_nc()
    res = run_bass_kernel_spmd(nc, in_maps, core_ids=list(range(8)))

    out = np.empty((4, 4096, D), np.float32)
    for core in range(8):
        seq, half = core // 2, core % 2
        o = res.results[core]["out"]
        o2 = np.asarray(o).transpose(1, 0, 2).reshape(D, NT).T
        out[seq, half * NT:(half + 1) * NT] = o2
    return out
